# version 26
# baseline (speedup 1.0000x reference)
import numpy as np
import ml_dtypes
import concourse.bass as bass
import concourse.mybir as mybir
from contextlib import ExitStack

F32 = mybir.dt.float32
BF16 = mybir.dt.bfloat16
AF = mybir.ActivationFunctionType
ALU = mybir.AluOpType
AX = mybir.AxisListType

E = 2048
H = 16
FF = 8192
NEG = -1e30
ESCALE = float(1.0 / np.sqrt(2048.0))
MARGIN = -20.0
CLIP_HI = float(128.0 - 1e-6)
CLIP_LO = float(-128.0 + 1e-6)


class Prog:

    ENGINES = ("sync", "gpsimd", "scalar", "vector", "tensor")

    def __init__(self, nc, stack):
        self.nc = nc
        self.q = {e: [] for e in self.ENGINES}
        self.sems = {}
        self.cnt = {}
        self.dirty = {e: False for e in self.ENGINES}
        for name in ("din", "dgp", "dout", "pe", "dve", "act", "pool"):
            self.sems[name] = stack.enter_context(nc.semaphore(name))
            self.cnt[name] = 0

    def emit(self, eng, fn, waits=(), inc=None, k=1):
        assert fn is not None or inc is None, "nop-inc is not completion-safe"
        if fn is not None:
            self.dirty[eng] = (inc is None)
        tok = None
        if inc is not None:
            self.cnt[inc] += k
            tok = (inc, self.cnt[inc])
        best = {}
        for w in waits:
            if w is None:
                continue
            grp = w if (isinstance(w, tuple) and w and isinstance(w[0], tuple)) else (w,)
            for x in grp:
                if x is None:
                    continue
                sname, v = x
                if v > best.get(sname, -1):
                    best[sname] = v
        waits = tuple(best.items())
        sems = self.sems

        def closure(e, waits=waits, fn=fn, inc=inc, k=k):
            for s, v in waits:
                e.wait_ge(sems[s], v)
            ins = fn(e) if fn is not None else None
            if inc is not None:
                if ins is None:
                    ins = e.nop()
                ins.then_inc(sems[inc], k)
        self.q[eng].append(closure)
        return tok

    def wait(self, eng, *toks):
        toks = [t for t in toks if t is not None]
        if toks:
            self.emit(eng, None, waits=toks)

    def dma(self, eng, out, in_, waits=(), **kw):
        sem = {"sync": "din", "gpsimd": "dgp", "scalar": "dout"}[eng]
        return self.emit(eng, lambda e: e.dma_start(out=out, in_=in_, **kw),
                         waits=waits, inc=sem, k=16)

    def boundary(self):
        return tuple((s, self.cnt[s]) for s in ("pe", "dve", "act", "din", "dgp", "dout"))

    def fence(self, engines=("tensor", "vector", "scalar")):
        m = {"tensor": "pe", "vector": "dve", "scalar": "act"}
        for e in engines:
            assert not self.dirty[e], f"fence with un-inc'd tail on {e}"
        toks = [(m[e], self.cnt[m[e]]) for e in engines]
        for e in engines:
            self.wait(e, *[t for t in toks if t[0] != m[e]])

    def finalize(self, block):
        for eng in self.ENGINES:
            closures = self.q[eng]
            fn = getattr(block, eng)

            def body(engine, closures=closures):
                for c in closures:
                    c(engine)
            fn(body)


def make_consts():
    a = np.arange(64)
    freqs = 10000.0 ** (-(2 * a) / 128.0)
    RT = np.zeros((128, H * 128), np.float32)
    for h in range(H):
        ang = h * freqs
        c, s = np.cos(ang), np.sin(ang)
        R = np.zeros((128, 128), np.float32)
        R[2 * a, 2 * a] = c
        R[2 * a, 2 * a + 1] = -s
        R[2 * a + 1, 2 * a] = s
        R[2 * a + 1, 2 * a + 1] = c
        RT[:, h * 128:(h + 1) * 128] = R.T
    stair = np.where(np.tril(np.ones((128, 128), bool)), 0.0, NEG).astype(np.float32)
    return {
        "c_rt": RT.astype(ml_dtypes.bfloat16),
        "c_idb": np.eye(128).astype(ml_dtypes.bfloat16),
        "c_idf": np.ones((1, 1), dtype=np.float32),
        "c_stair": stair.astype(ml_dtypes.bfloat16),
        "c_stairT": np.ascontiguousarray(stair.T).astype(ml_dtypes.bfloat16),
        "c_onec_bf": np.ones((128, 1), ml_dtypes.bfloat16),
        "c_oner_bf": np.ones((1, 128), ml_dtypes.bfloat16),
        "c_onec_f": np.ones((128, 1), np.float32),
        "c_oner_f": np.ones((1, 128), np.float32),
    }


def build(T=2048, upto="F", debug=False):
    NS = T // 512
    NTC = T // 128
    nc = bass.Bass("TRN2", target_bir_lowering=False)

    x_h = nc.dram_tensor("x", [T, E], F32, kind="ExternalInput").ap()
    wattn_h = nc.dram_tensor("w_attn", [E, 3 * E], F32, kind="ExternalInput").ap()
    battn_h = nc.dram_tensor("b_attn", [1, 3 * E], F32, kind="ExternalInput").ap()
    wproj_h = nc.dram_tensor("w_proj", [E, E], F32, kind="ExternalInput").ap()
    bproj_h = nc.dram_tensor("b_proj", [1, E], F32, kind="ExternalInput").ap()
    wfcT_h = nc.dram_tensor("wfcT", [E, FF], F32, kind="ExternalInput").ap()
    wprT_h = nc.dram_tensor("wprT", [FF, E], F32, kind="ExternalInput").ap()
    y_h = nc.dram_tensor("y", [T, E], F32, kind="ExternalOutput").ap()

    sk = "ExternalOutput" if debug else "Internal"
    tfc_h = nc.dram_tensor("s_tfc", [E, FF], BF16, kind=sk).ap()
    tpr_h = nc.dram_tensor("s_tpr", [16, 128, FF], BF16, kind=sk).ap()
    o_h = nc.dram_tensor("s_o", [T // 128, 128, E], BF16, kind=sk).ap()
    x2_h = nc.dram_tensor("s_x2", [T, E], F32, kind=sk).ap()
    ws_h = nc.dram_tensor("s_ws", [1, E], F32, kind=sk).ap()

    cap = {k: nc.inline_tensor(v, k).ap() for k, v in make_consts().items()}

    st = ExitStack()
    with st:
        en = st.enter_context
        p = Prog(nc, st)

        pb = [en(nc.psum_tensor(f"pb{i}", [128, 512], F32)) for i in range(8)]
        pbh = [b[:].bitcast(BF16) for b in pb]

        rt_sb = en(nc.sbuf_tensor("rt_sb", [128, H * 128], BF16))
        idb_sb = en(nc.sbuf_tensor("idb_sb", [128, 128], BF16))
        idf_sb = en(nc.sbuf_tensor("idf_sb", [1, 1], F32))
        stair_sb = en(nc.sbuf_tensor("stair_sb", [128, 128], BF16))
        stairT_sb = en(nc.sbuf_tensor("stairT_sb", [128, 128], BF16))
        onec_bf = en(nc.sbuf_tensor("onec_bf", [128, 1], BF16))
        oner_bf = en(nc.sbuf_tensor("oner_bf", [1, 128], BF16))
        onec_f = en(nc.sbuf_tensor("onec_f", [128, 1], F32))
        oner_f = en(nc.sbuf_tensor("oner_f", [1, 128], F32))
        bcols = en(nc.sbuf_tensor("bcols", [128, 32], F32))
        bvrow = en(nc.sbuf_tensor("bvrow", [1, E], BF16))
        eps5 = en(nc.sbuf_tensor("eps5", [128, 1], F32))
        eps6 = en(nc.sbuf_tensor("eps6", [128, 1], F32))
        st6 = en(nc.sbuf_tensor("st6", [128, 24], F32))
        mv = en(nc.sbuf_tensor("mv", [128, 2], F32))
        rstd = en(nc.sbuf_tensor("rstd", [128, 1], F32))
        mb = en(nc.sbuf_tensor("mb", [128, 1], F32))
        tmp1 = en(nc.sbuf_tensor("tmp1", [128, 1], F32))
        tmp2 = en(nc.sbuf_tensor("tmp2", [128, 1], F32))
        gam = en(nc.sbuf_tensor("gam", [128, 4], F32))
        nth = en(nc.sbuf_tensor("nth", [128, 2], F32))
        gsc = en(nc.sbuf_tensor("gsc", [1, 2], F32))
        acc = en(nc.sbuf_tensor("acc", [128, 2], F32))
        red2 = en(nc.sbuf_tensor("red2", [128, 2], F32))

        ck = {}
        for dst, key in [(rt_sb, "c_rt"), (idb_sb, "c_idb"), (idf_sb, "c_idf"),
                         (stair_sb, "c_stair"), (stairT_sb, "c_stairT"),
                         (onec_bf, "c_onec_bf"), (oner_bf, "c_oner_bf"),
                         (onec_f, "c_onec_f"), (oner_f, "c_oner_f")]:
            ck[key] = p.dma("sync", dst[:], cap[key])
        p.emit("vector", lambda e: e.memset(eps5[:], 1e-5), inc="dve")
        p.emit("vector", lambda e: e.memset(eps6[:], 1e-6), inc="dve")

        def ln_tail(mean_ap, var_ap, eps_ap, waits):
            a1 = p.emit("scalar", lambda e: e.activation(
                tmp1[:], var_ap, AF.Ln, bias=eps_ap, scale=1.0), waits=waits, inc="act")
            a2 = p.emit("scalar", lambda e: e.activation(
                rstd[:], tmp1[:], AF.Exp, bias=0.0, scale=-0.5), waits=[a1], inc="act")
            return p.emit("vector", lambda e: e.tensor_scalar(
                mb[:], mean_ap, rstd[:], -1.0, ALU.mult, ALU.mult), waits=[a2], inc="dve")

        def ln_stats(src_tile, W, waits):
            nblk = W // 512
            s1 = None
            for i in range(nblk):
                s1 = p.emit("vector", lambda e, i=i: e.bn_stats(
                    st6[:, i * 6:(i + 1) * 6], src_tile[:, i * 512:(i + 1) * 512]),
                    waits=(list(waits) if i == 0 else [s1]), inc="dve")
            return p.emit("vector", lambda e, nblk=nblk: e.bn_aggr(
                mv[:], st6[:, 0:6 * nblk]), waits=[s1], inc="dve")

        with ExitStack() as stW:
            brow = stW.enter_context(nc.sbuf_tensor("brow", [1, 3 * E], F32))
            t_b = p.dma("sync", brow[:], battn_h)
            prev_copy = None
            for j in range(32):
                tk = p.emit("tensor", lambda e, j=j: e.transpose(
                    pb[0][:, 0:1], brow[0:1, j * 128:(j + 1) * 128], idf_sb[:]),
                    waits=([t_b, ck["c_idf"]] if j == 0 else []) + [prev_copy],
                    inc="pe")
                prev_copy = p.emit("vector", lambda e, j=j: e.tensor_copy(
                    bcols[:, j:j + 1], pb[0][:, 0:1]), waits=[tk], inc="dve")
            bv = p.emit("vector", lambda e: e.tensor_copy(bvrow[:], brow[0:1, 2 * E:3 * E]),
                        waits=[t_b, prev_copy], inc="dve")
        z0 = p.emit("vector", lambda e: e.memset(acc[:], 0.0), waits=[bv], inc="dve")
        dve_last = z0
        w0_pe = ("pe", p.cnt["pe"])

        with ExitStack() as stA:
            enA = stA.enter_context
            K_sb = enA(nc.sbuf_tensor("K_sb", [128, H * T], BF16))
            V_sb = enA(nc.sbuf_tensor("V_sb", [128, NTC * E], BF16))
            Q_sb = enA(nc.sbuf_tensor("Q_sb", [128, H * 512], BF16))
            xn_sb = enA(nc.sbuf_tensor("xn_sb", [128, 16 * 512], BF16))
            spareA = enA(nc.sbuf_tensor("spareA", [128, 128], BF16))
            xt_sb = enA(nc.sbuf_tensor("xt_sb", [128, E], F32))
            wblk = [enA(nc.sbuf_tensor(f"wblk{i}", [128, 16 * 256], BF16)) for i in range(2)]
            E_sb = [enA(nc.sbuf_tensor(f"E_sb{i}", [128, 512], BF16)) for i in range(2)]
            o_sb = [enA(nc.sbuf_tensor(f"o_sb{i}", [128, 512], BF16)) for i in range(4)]
            sbc_sb = [enA(nc.sbuf_tensor(f"sbc_sb{i}", [128, 512], BF16)) for i in range(2)]
            negm4 = enA(nc.sbuf_tensor("negm4", [128, 4], BF16))
            negrow = [enA(nc.sbuf_tensor(f"negrow{i}", [1, 512], BF16)) for i in range(2)]
            sinvb = [enA(nc.sbuf_tensor(f"sinvb{i}", [1, 512], BF16)) for i in range(2)]

            B0 = p.boundary()
            xt_free = B0
            prev_m1 = None
            wblk_free = [B0, B0]
            o_free = [None, None, None, None]
            ln_apply_tok = [None] * 4
            xn_read_done = B0

            def ln_block_tau(s, tau):
                nonlocal xt_free, prev_m1, xn_read_done
                tc = s * 4 + tau
                xk = p.dma("sync", xt_sb[:], x_h[tc * 128:(tc + 1) * 128, :],
                           waits=[xt_free])
                s2 = ln_stats(xt_sb, E, [xk, prev_m1])
                m1 = ln_tail(mv[:, 0:1], mv[:, 1:2], eps5[:], [s2])
                prev_m1 = m1
                ak = p.emit("scalar", lambda e, tau=tau: e.activation(
                    xn_sb[:, tau * 2048:(tau + 1) * 2048], xt_sb[:],
                    AF.Identity, bias=mb[:], scale=rstd[:]),
                    waits=[m1, xn_read_done if tau == 0 else None], inc="act")
                ln_apply_tok[tau] = ak
                xt_free = ak

            NB_A = 64
            dstA = [(b % 16) * 4 + b // 16 for b in range(NB_A)]
            cycA, fixA = [], []
            seenA = [False] * NB_A
            for b0 in range(NB_A):
                if seenA[b0]:
                    continue
                if dstA[b0] == b0:
                    fixA.append(b0); seenA[b0] = True; continue
                cyc = [b0]; seenA[b0] = True; nxt = dstA[b0]
                while nxt != b0:
                    cyc.append(nxt); seenA[nxt] = True; nxt = dstA[nxt]
                cycA.append(cyc)

            for tau in range(4):
                ln_block_tau(0, tau)

            for s in range(NS):
                axn_copies = []

                def amove(srcb, dstb, extra=()):
                    bk_ = 2 + (len(axn_copies) % 4)
                    trk = p.emit("tensor", lambda e, srcb=srcb, bk_=bk_: e.transpose(
                        pbh[bk_][:, 0:128], xn_sb[:, srcb * 128:(srcb + 1) * 128],
                        idb_sb[:]),
                        waits=list(extra) +
                              ([axn_copies[-4]] if len(axn_copies) >= 4 else []),
                        inc="pe")
                    axn_copies.append(p.emit(
                        "vector", lambda e, dstb=dstb, bk_=bk_: e.tensor_copy(
                            xn_sb[:, dstb * 128:(dstb + 1) * 128], pbh[bk_][:, 0:128]),
                        waits=[trk], inc="dve"))

                first_wA = list(ln_apply_tok) + [ck["c_idb"]]
                for b0 in fixA:
                    amove(b0, b0, first_wA if not axn_copies else ())
                for cyc in cycA:
                    blast = cyc[-1]
                    bk_ = 2 + (len(axn_copies) % 4)
                    trk = p.emit("tensor", lambda e, blast=blast, bk_=bk_: e.transpose(
                        pbh[bk_][:, 0:128], xn_sb[:, blast * 128:(blast + 1) * 128],
                        idb_sb[:]),
                        waits=(first_wA if not axn_copies else []) +
                              ([axn_copies[-4]] if len(axn_copies) >= 4 else []),
                        inc="pe")
                    sv = p.emit("vector", lambda e, bk_=bk_: e.tensor_copy(
                        spareA[:], pbh[bk_][:, 0:128]), waits=[trk], inc="dve")
                    axn_copies.append(sv)
                    for i in range(len(cyc) - 1, 0, -1):
                        amove(cyc[i - 1], cyc[i])
                    axn_copies.append(p.emit(
                        "vector", lambda e, b0=cyc[0]: e.tensor_copy(
                            xn_sb[:, b0 * 128:(b0 + 1) * 128], spareA[:]),
                        waits=[axn_copies[-1]], inc="dve"))
                xn_ready = axn_copies[-1]
                if upto == "Aln":
                    break

                psq_free = [axn_copies[-2], axn_copies[-1]]
                rope_free = [axn_copies[-4], axn_copies[-3]]
                pend_rope = [None, None]
                esb_free = [None, None]
                psv_free = [None, None, None, None]
                for b in range(24):
                    wk = p.dma("gpsimd", wblk[b % 2][:].rearrange("p (c j) -> p c j", j=256),
                               wattn_h[:, b * 256:(b + 1) * 256].rearrange("(c p) j -> p c j", p=128),
                               waits=[wblk_free[b % 2]])
                    if b < 16:
                        for jj in range(2):
                            j = 2 * b + jj
                            h = j % 16
                            for ic in range(16):
                                p.emit("tensor", lambda e, b=b, jj=jj, ic=ic: e.matmul(
                                    pb[2 + jj][:], wblk[b % 2][:, ic * 256 + jj * 128: ic * 256 + (jj + 1) * 128],
                                    xn_sb[:, ic * 512:(ic + 1) * 512],
                                    start=(ic == 0), stop=(ic == 15)),
                                    waits=([wk, xn_ready, psq_free[jj]] if ic == 0 else []),
                                    inc=("pe" if ic == 15 else None))
                            mm = ("pe", p.cnt["pe"])
                            ev1 = p.emit("scalar", lambda e, jj=jj, j=j: e.activation(
                                E_sb[jj][:], pb[2 + jj][:], AF.Identity,
                                bias=bcols[:, j:j + 1], scale=1.0),
                                waits=[mm, esb_free[jj]], inc="act")
                            psq_free[jj] = ev1
                            if pend_rope[jj] is not None:
                                pend_rope[jj]()
                                pend_rope[jj] = None

                            def do_rope(jj=jj, h=h, j=j, b=b, ev1=ev1):
                                rk = p.emit("tensor", lambda e, jj=jj, h=h: e.matmul(
                                    pb[4 + jj][:], rt_sb[:, h * 128:(h + 1) * 128],
                                    E_sb[jj][:], start=True, stop=True),
                                    waits=[ev1, ck["c_rt"] if b == 0 else None,
                                           rope_free[jj]], inc="pe")
                                esb_free[jj] = ("pe", p.cnt["pe"])
                                if j < 16:
                                    dst = Q_sb[:, h * 512:(h + 1) * 512]
                                else:
                                    dst = K_sb[:, h * T + s * 512: h * T + (s + 1) * 512]
                                ev2 = p.emit("vector", lambda e, jj=jj, dst=dst: e.tensor_copy(
                                    dst, pb[4 + jj][:]), waits=[rk], inc="dve")
                                rope_free[jj] = ev2
                            pend_rope[jj] = do_rope
                    else:
                        for jj in range(2):
                            if pend_rope[jj] is not None:
                                pend_rope[jj]()
                                pend_rope[jj] = None
                        vb = b - 16
                        for tau in range(4):
                            bi = 2 + tau
                            for ic in range(16):
                                p.emit("tensor", lambda e, b=b, tau=tau, ic=ic, bi=bi: e.matmul(
                                    pb[bi][:, 0:256],
                                    xn_sb[:, ic * 512 + tau * 128: ic * 512 + (tau + 1) * 128],
                                    wblk[b % 2][:, ic * 256:(ic + 1) * 256],
                                    start=(ic == 0), stop=False),
                                    waits=([wk, psv_free[tau],
                                            psq_free[0] if b == 16 and tau == 0 else None,
                                            psq_free[1] if b == 16 and tau == 1 else None,
                                            rope_free[0] if b == 16 and tau == 2 else None,
                                            rope_free[1] if b == 16 and tau == 3 else None]
                                           if ic == 0 else []),
                                    inc=None)
                            p.emit("tensor", lambda e, bi=bi, vb=vb: e.matmul(
                                pb[bi][:, 0:256], oner_bf[0:1, :],
                                bvrow[0:1, vb * 256:(vb + 1) * 256],
                                start=False, stop=True), inc="pe")
                            mm = ("pe", p.cnt["pe"])
                            ev = p.emit("vector", lambda e, bi=bi, s=s, tau=tau, vb=vb: e.tensor_copy(
                                V_sb[:, (s * 4 + tau) * E + vb * 256:(s * 4 + tau) * E + (vb + 1) * 256],
                                pb[bi][:, 0:256]), waits=[mm], inc="dve")
                            psv_free[tau] = ev
                    wblk_free[b % 2] = ("pe", p.cnt["pe"])
                xn_read_done = ("pe", p.cnt["pe"])
                p.fence(("tensor", "vector", "scalar"))
                if debug and s == 0:
                    dbg_q = nc.dram_tensor("d_q", [128, H * 512], BF16, kind="ExternalOutput").ap()
                    dbg_k = nc.dram_tensor("d_k", [128, H * T], BF16, kind="ExternalOutput").ap()
                    dbg_v = nc.dram_tensor("d_v", [128, NTC * E], BF16, kind="ExternalOutput").ap()
                    dbg_xn = nc.dram_tensor("d_xn", [128, 16 * 512], BF16, kind="ExternalOutput").ap()
                    p.dma("scalar", dbg_q, Q_sb[:])
                    p.dma("scalar", dbg_k, K_sb[:])
                    p.dma("scalar", dbg_v, V_sb[:])
                    p.dma("scalar", dbg_xn, xn_sb[:])
                    p.wait("scalar", ("dout", p.cnt["dout"]))
                if upto in ("Aq", "Aln"):
                    break

                psmax_free = [None, None]
                red_tok = [None] * 4
                negm_free = [None] * 4
                negrow_ready = [None, None]
                negrow_free = [None, None]
                esb_free = [None, None]
                exp_tok = [None, None]
                sbc_free = [None, None]
                sum_free = [None, None]
                sinvb_free = None
                o_ps_free = [None, None]
                negtr_free = [None, None]

                def plan_a1(h):
                    for tau in range(4):
                        g = 4 * s + tau
                        Ls = 128 * min((g + 2) // 2, 4)
                        bi = tau % 2
                        p.emit("tensor", lambda e, h=h, tau=tau, Ls=Ls, bi=bi: e.matmul(
                            pb[bi][:, 0:Ls],
                            Q_sb[:, h * 512 + tau * 128: h * 512 + (tau + 1) * 128],
                            K_sb[:, h * T: h * T + Ls],
                            start=True, stop=True),
                            waits=[psmax_free[bi]], inc="pe")
                        mm = ("pe", p.cnt["pe"])
                        mk = None
                        if g == 0:
                            mk = p.emit("vector", lambda e: e.tensor_tensor(
                                pb[0][:, 0:128], pb[0][:, 0:128], stair_sb[:], ALU.add),
                                waits=[mm, ck["c_stair"]], inc="dve")
                        red_tok[tau] = p.emit("vector", lambda e, Ls=Ls, bi=bi, tau=tau: e.tensor_reduce(
                            negm4[:, tau:tau + 1], pb[bi][:, 0:Ls], AX.X, ALU.max, negate=True),
                            waits=[mm, mk, negm_free[tau]], inc="dve")
                        psmax_free[bi] = red_tok[tau]

                def plan_a2(h):
                    last = None
                    for tau in range(4):
                        bi = tau % 2
                        trk = p.emit("tensor", lambda e, tau=tau, bi=bi: e.transpose(
                            pbh[bi][0:1, 512 + tau * 128: 512 + (tau + 1) * 128],
                            negm4[:, tau:tau + 1], idb_sb[:]),
                            waits=[red_tok[tau], red_tok[bi + 2], negtr_free[bi]],
                            inc="pe")
                        negm_free[tau] = trk
                        last = p.emit("vector", lambda e, h=h, tau=tau, bi=bi: e.tensor_scalar(
                            negrow[h % 2][0:1, tau * 128:(tau + 1) * 128],
                            pbh[bi][0:1, 512 + tau * 128: 512 + (tau + 1) * 128],
                            MARGIN, None, ALU.add),
                            waits=[trk, negrow_free[h % 2] if tau == 0 else None], inc="dve")
                        negtr_free[bi] = last
                        psmax_free[bi] = last
                    negrow_ready[h % 2] = last

                pend_tail = [None]

                def plan_b(h):
                    nonlocal sinvb_free
                    nk = 4 * s + 4
                    ob_ = 4 + h % 2
                    sb_ = 6 + h % 2

                    def sum_av(kc):
                        off = max(0, (kc - 4 * s) * 128)
                        sl = kc % 2
                        p.emit("tensor", lambda e, kc=kc, off=off, sl=sl: e.matmul(
                            pb[sb_][0:1, off:512], onec_bf[:], E_sb[sl][:, off:512],
                            start=(kc == 0), stop=(kc == nk - 1)),
                            waits=[exp_tok[sl], sum_free[h % 2] if kc == 0 else None],
                            inc=None)
                        p.emit("tensor", lambda e, h=h, kc=kc, off=off, sl=sl: e.matmul(
                            pb[ob_][:, off:512], V_sb[:, kc * E + h * 128: kc * E + (h + 1) * 128],
                            E_sb[sl][:, off:512],
                            start=(kc == 0), stop=(kc == nk - 1)),
                            waits=[o_ps_free[h % 2]] if kc == 0 else [], inc="pe")
                        esb_free[sl] = ("pe", p.cnt["pe"])

                    for kc in range(nk):
                        off = max(0, (kc - 4 * s) * 128)
                        sl = kc % 2
                        p.emit("tensor", lambda e, h=h, kc=kc, off=off, sl=sl: e.matmul(
                            pb[2 + sl][:, off:512],
                            K_sb[:, h * T + kc * 128: h * T + (kc + 1) * 128],
                            Q_sb[:, h * 512 + off:(h + 1) * 512],
                            start=True, stop=False),
                            waits=[exp_tok[sl],
                                   negrow_ready[h % 2] if kc == 0 else None],
                            inc=None)
                        nmk = p.emit("tensor", lambda e, h=h, off=off, sl=sl: e.matmul(
                            pb[2 + sl][:, off:512], oner_bf[0:1, :],
                            negrow[h % 2][0:1, off:512], start=False, stop=True),
                            inc="pe")
                        mk = None
                        if kc >= 4 * s:
                            mk = p.emit("vector", lambda e, off=off, sl=sl: e.tensor_tensor(
                                pb[2 + sl][:, off:off + 128], pb[2 + sl][:, off:off + 128],
                                stairT_sb[:], ALU.add), waits=[nmk], inc="dve")
                        ek = p.emit("scalar", lambda e, off=off, sl=sl: e.activation(
                            E_sb[sl][:, off:512], pb[2 + sl][:, off:512], AF.Exp,
                            bias=0.0, scale=1.0), waits=[nmk, mk, esb_free[sl]], inc="act")
                        exp_tok[sl] = ek
                        if kc == 1 and pend_tail[0] is not None:
                            pend_tail[0]()
                            pend_tail[0] = None
                        if kc >= 1:
                            sum_av(kc - 1)
                    sum_av(nk - 1)
                    mm_last = ("pe", p.cnt["pe"])
                    def tail(h=h, s=s, ob_=ob_, sb_=sb_, mm_last=mm_last):
                        nonlocal sinvb_free
                        r1 = p.emit("vector", lambda e: e.reciprocal(
                            pb[sb_][0:1, :], pb[sb_][0:1, :]), waits=[mm_last], inc="dve")
                        r2 = p.emit("vector", lambda e: e.tensor_scalar(
                            sinvb[h % 2][0:1, :], pb[sb_][0:1, :], ESCALE, None, ALU.mult),
                            waits=[r1, sinvb_free], inc="dve")
                        bk2 = p.emit("tensor", lambda e: e.matmul(
                            pb[sb_][:], oner_bf[0:1, :], sinvb[h % 2][0:1, :],
                            start=True, stop=True), waits=[r2], inc="pe")
                        sinvb_free = bk2
                        c1 = p.emit("vector", lambda e: e.tensor_copy(
                            sbc_sb[h % 2][:], pb[sb_][:]), waits=[bk2, sbc_free[h % 2]],
                            inc="dve")
                        sum_free[h % 2] = c1
                        om = p.emit("vector", lambda e: e.tensor_tensor(
                            o_sb[h % 4][:], pb[ob_][:], sbc_sb[h % 2][:], ALU.mult),
                            waits=[c1, o_free[h % 4]], inc="dve")
                        o_ps_free[h % 2] = om
                        sbc_free[h % 2] = om
                        dk = p.dma("scalar",
                                   o_h[s * 4:(s + 1) * 4, :, h * 128:(h + 1) * 128]
                                   .rearrange("c p j -> p c j"),
                                   o_sb[h % 4][:].rearrange("p (c j) -> p c j", j=128),
                                   waits=[om])
                        o_free[h % 4] = dk
                    pend_tail[0] = tail
                    negrow_free[h % 2] = ("pe", p.cnt["pe"])

                for h in range(H):
                    plan_a1(h)
                    if h > 0:
                        plan_b(h - 1)
                    plan_a2(h)
                    if s + 1 < NS and h in (4, 7, 10, 13):
                        ln_block_tau(s + 1, (h - 4) // 3)
                plan_b(H - 1)
                if pend_tail[0] is not None:
                    pend_tail[0]()
                    pend_tail[0] = None
                p.fence(("tensor", "vector", "scalar"))
        if upto in ("A", "Aq", "Aln"):
            p.wait("scalar", ("dout", p.cnt["dout"]))
            with nc.Block() as block:
                p.finalize(block)
            return nc

        with ExitStack() as stD:
            enD = stD.enter_context
            wp_sb = enD(nc.sbuf_tensor("wp_sb", [128, 16 * E], BF16))
            osb = [enD(nc.sbuf_tensor(f"osb{i}", [128, 16 * 128], BF16)) for i in range(2)]
            ao_sb = [enD(nc.sbuf_tensor(f"ao_sb{i}", [128, E], F32)) for i in range(2)]
            junk = [enD(nc.sbuf_tensor(f"junkD{i}", [128, E], F32)) for i in range(2)]
            b_pb = enD(nc.sbuf_tensor("b_pb", [128, E], F32))
            bprow = enD(nc.sbuf_tensor("bprow", [1, E], F32))
            xd_sb = enD(nc.sbuf_tensor("xd_sb", [128, E], F32))
            x2w_sb = enD(nc.sbuf_tensor("x2w_sb", [128, E], F32))
            tw1 = [enD(nc.sbuf_tensor(f"tw1_{i}", [128, 2048], F32)) for i in range(4)]
            psD = [[pb[0], pb[1], pb[2], pb[3]], [pb[4], pb[5], pb[6], pb[7]]]

            B0 = p.boundary()

            t1_tiles = []
            for r_ in range(E // 128):
                for cp_ in range(4):
                    t1_tiles.append((wfcT_h[r_ * 128:(r_ + 1) * 128,
                                            cp_ * 2048:(cp_ + 1) * 2048], 0))
            for r_ in range(FF // 128):
                t1_tiles.append((wprT_h[r_ * 128:(r_ + 1) * 128, :], 1))
            t1_i = [0]
            tw1_free = [[B0], [B0], [B0], [B0]]

            def t1_pump(n):
                nonlocal dve_last
                for _ in range(n):
                    if t1_i[0] >= len(t1_tiles):
                        return
                    src, col = t1_tiles[t1_i[0]]
                    j = t1_i[0] % 4
                    tk = p.dma("gpsimd", tw1[j][:], src, waits=tw1_free[j])
                    rk = p.emit("vector", lambda e, j=j: e.tensor_reduce(
                        red2[:, j % 2:j % 2 + 1], tw1[j][:], AX.X, ALU.add,
                        apply_absolute_value=True),
                        waits=[tk, dve_last], inc="dve")
                    tw1_free[j] = [rk]
                    dve_last = p.emit("vector", lambda e, col=col, j=j: e.tensor_tensor(
                        acc[:, col:col + 1], acc[:, col:col + 1], red2[:, j % 2:j % 2 + 1],
                        ALU.add), waits=[rk], inc="dve")
                    t1_i[0] += 1
            bp = p.dma("sync", bprow[:], bproj_h, waits=[B0])
            bb = None
            for c4 in range(4):
                bb = p.emit("tensor", lambda e, c4=c4: e.matmul(
                    psD[0][c4][:], oner_f[0:1, :],
                    bprow[0:1, c4 * 512:(c4 + 1) * 512], start=True, stop=True),
                    waits=[bp] if c4 == 0 else [], inc="pe")
            bpc = None
            for c4 in range(4):
                bpc = p.emit("vector", lambda e, c4=c4: e.tensor_copy(
                    b_pb[:, c4 * 512:(c4 + 1) * 512], psD[0][c4][:]), waits=[bb], inc="dve")
            wpk = None
            for dc in range(16):
                wpk = p.dma("gpsimd", wp_sb[:, dc * E:(dc + 1) * E],
                            wproj_h[dc * 128:(dc + 1) * 128, :],
                            waits=[B0] if dc == 0 else [])
            osb_free = [B0, B0]
            ps_free = [bpc, None]
            prev_x2 = None
            ap_hist = [bpc, None]
            ad_hist = [None, None]
            o_writes_done = ("dout", p.cnt["dout"])
            for tc in range(NTC):
                sl = tc % 2
                ok = p.dma("sync", osb[sl][:], o_h[tc],
                           waits=[osb_free[sl], o_writes_done if tc < 2 else None])
                for c4 in range(4):
                    for dc in range(16):
                        p.emit("tensor", lambda e, sl=sl, c4=c4, dc=dc: e.matmul(
                            psD[sl][c4][:],
                            osb[sl][:, dc * 128:(dc + 1) * 128],
                            wp_sb[:, dc * E + c4 * 512: dc * E + (c4 + 1) * 512],
                            start=(dc == 0), stop=(dc == 15)),
                            waits=([ok, wpk, ps_free[sl]] if (c4 == 0 and dc == 0) else []),
                            inc=("pe" if (c4 == 3 and dc == 15) else None))
                mm = ("pe", p.cnt["pe"])
                osb_free[sl] = mm
                ev = None
                for c4 in range(4):
                    ev = p.emit("vector", lambda e, sl=sl, c4=c4: e.scalar_tensor_tensor(
                        ao_sb[sl][:, c4 * 512:(c4 + 1) * 512], psD[sl][c4][:], 1.0,
                        b_pb[:, c4 * 512:(c4 + 1) * 512], ALU.mult, ALU.add,
                        accum_out=st6[:, c4:c4 + 1]),
                        waits=[mm, ev, ap_hist[sl] if c4 == 0 else None], inc="dve")
                ps_free[sl] = ev
                sm = p.emit("vector", lambda e: e.tensor_reduce(
                    mv[:, 0:1], st6[:, 0:4], AX.X, ALU.add), waits=[ev], inc="dve")
                sq = None
                for c4 in range(4):
                    sq = p.emit("scalar", lambda e, sl=sl, c4=c4: e.activation(
                        junk[sl][:, c4 * 512:(c4 + 1) * 512], ao_sb[sl][:, c4 * 512:(c4 + 1) * 512],
                        AF.Square, bias=0.0, scale=1.0,
                        accum_out=st6[:, 4 + c4:5 + c4]),
                        waits=([ev, ad_hist[sl]] if c4 == 0 else [sq]), inc="act")
                sm2 = p.emit("vector", lambda e: e.tensor_reduce(
                    tmp2[:], st6[:, 4:8], AX.X, ALU.add), waits=[sq, sm], inc="dve")
                k1 = p.emit("vector", lambda e: e.tensor_scalar(
                    mv[:, 0:1], mv[:, 0:1], 1.0 / E, None, ALU.mult), waits=[sm2], inc="dve")
                k2 = p.emit("vector", lambda e: e.tensor_scalar(
                    tmp2[:], tmp2[:], 1.0 / E, None, ALU.mult), waits=[k1], inc="dve")
                k3 = p.emit("vector", lambda e: e.tensor_scalar(
                    mv[:, 1:2], mv[:, 0:1], mv[:, 0:1], -1.0, ALU.mult, ALU.mult),
                    waits=[k2], inc="dve")
                k4 = p.emit("vector", lambda e: e.tensor_tensor(
                    mv[:, 1:2], mv[:, 1:2], tmp2[:], ALU.add), waits=[k3], inc="dve")
                m1 = ln_tail(mv[:, 0:1], mv[:, 1:2], eps5[:], [k4])
                ap1 = p.emit("scalar", lambda e, sl=sl: e.activation(
                    junk[sl][:], ao_sb[sl][:], AF.Identity, bias=mb[:], scale=rstd[:]),
                    waits=[m1, sq], inc="act")
                ap_hist[sl] = ap1
                xk = p.dma("sync", xd_sb[:], x_h[tc * 128:(tc + 1) * 128, :],
                           waits=[prev_x2 if prev_x2 else B0])
                ad = p.emit("vector", lambda e, sl=sl: e.tensor_tensor(
                    x2w_sb[:], junk[sl][:], xd_sb[:], ALU.add), waits=[ap1, xk], inc="dve")
                ad_hist[sl] = ad
                dk = p.dma("scalar", x2_h[tc * 128:(tc + 1) * 128, :], x2w_sb[:], waits=[ad])
                prev_x2 = dk
                t1_pump(8)
            t1_pump(10 ** 9)
            p.fence(("tensor", "vector", "scalar"))

        with ExitStack() as stT:
            enT = stT.enter_context
            tw = [enT(nc.sbuf_tensor(f"tw{i}", [128, 4096], F32)) for i in range(4)]
            ta = [enT(nc.sbuf_tensor(f"ta{i}", [128, 4096], BF16)) for i in range(3)]
            tb = [enT(nc.sbuf_tensor(f"tb{i}", [128, 4096], BF16)) for i in range(3)]
            tob = [enT(nc.sbuf_tensor(f"tob{i}", [128, 4096], BF16)) for i in range(3)]
            wrow = enT(nc.sbuf_tensor("wrow", [1, E], F32))

            B0t = p.boundary()
            p.emit("tensor", lambda e: e.matmul(pb[0][0:1, 0:1], acc[:, 0:1],
                                                onec_f[:], start=True, stop=True),
                   waits=[dve_last, ck["c_onec_f"]], inc=None)
            tk = p.emit("tensor", lambda e: e.matmul(pb[0][0:1, 128:129], acc[:, 1:2],
                                                     onec_f[:], start=True, stop=True),
                        inc="pe")
            g1a = p.emit("vector", lambda e: e.tensor_copy(
                gsc[0:1, 0:1], pb[0][0:1, 0:1]), waits=[tk], inc="dve")
            g1 = p.emit("vector", lambda e: e.tensor_copy(
                gsc[0:1, 1:2], pb[0][0:1, 128:129]), waits=[g1a], inc="dve")
            g2 = p.emit("vector", lambda e: e.tensor_scalar(
                gsc[0:1, 0:2], gsc[0:1, 0:2], 1.0 / (E * FF), None, ALU.mult),
                waits=[g1], inc="dve")
            bk = p.emit("tensor", lambda e: e.matmul(pb[1][:, 0:2], oner_f[0:1, :],
                                                     gsc[0:1, 0:2], start=True, stop=True),
                        waits=[g2, ck["c_oner_f"]], inc="pe")
            g3 = p.emit("vector", lambda e: e.tensor_copy(gam[:, 0:2], pb[1][:, 0:2]),
                        waits=[bk], inc="dve")
            g4 = p.emit("vector", lambda e: e.tensor_scalar(
                gam[:, 2:4], gam[:, 0:2], 1.0 / 256.0, None, ALU.mult), waits=[g3], inc="dve")
            g5 = p.emit("vector", lambda e: e.tensor_scalar(
                gam[:, 0:2], gam[:, 0:2], 1e-6, 0.5, ALU.add, ALU.mult), waits=[g4], inc="dve")
            g6 = p.emit("vector", lambda e: e.tensor_scalar(
                nth[:, 0:2], gam[:, 0:2], -1.0, None, ALU.mult), waits=[g5], inc="dve")

            BIG = 1e30
            ti = 0
            tw_free = [[B0t]] * 4
            ta_free = [None, None, None]
            tb_free = [None, None, None]
            tob_free = [[B0t], [B0t], [B0t]]

            def tern(src_ap, dst_fn, width, col):
                nonlocal ti
                buf = tw[ti % 4]
                tk = p.dma("sync", buf[:, 0:width], src_ap, waits=tw_free[ti % 4])
                a1 = p.emit("scalar", lambda e, buf=buf, width=width, col=col, ti=ti: e.activation(
                    ta[ti % 3][:, 0:width], buf[:, 0:width], AF.Sign,
                    bias=nth[:, col:col + 1], scale=1.0),
                    waits=[tk, g6, ta_free[ti % 3]], inc="act")
                if ti % 4 == 0:
                    b2 = p.emit("scalar", lambda e, buf=buf, width=width, col=col, ti=ti: e.activation(
                        tb[ti % 3][:, 0:width], buf[:, 0:width], AF.Sign,
                        bias=gam[:, col:col + 1], scale=1.0),
                        waits=[tb_free[ti % 3]], inc="act")
                    tw_free[ti % 4] = [b2]
                else:
                    b1 = p.emit("vector", lambda e, buf=buf, width=width, col=col, ti=ti: e.tensor_scalar(
                        tb[ti % 3][:, 0:width], buf[:, 0:width], gam[:, col:col + 1], BIG,
                        ALU.add, ALU.mult),
                        waits=[tk, g6, tb_free[ti % 3]], inc="dve")
                    b2 = p.emit("vector", lambda e, width=width, ti=ti: e.tensor_scalar(
                        tb[ti % 3][:, 0:width], tb[ti % 3][:, 0:width], 1.0, -1.0,
                        ALU.min, ALU.max), waits=[b1], inc="dve")
                    tw_free[ti % 4] = [a1, b1]
                a3 = p.emit("vector", lambda e, width=width, ti=ti: e.tensor_tensor(
                    tob[ti % 3][:, 0:width], ta[ti % 3][:, 0:width],
                    tb[ti % 3][:, 0:width], ALU.add),
                    waits=[a1, b2] + tob_free[ti % 3], inc="dve")
                ta_free[ti % 3] = a3
                tb_free[ti % 3] = a3
                tob_free[ti % 3] = dst_fn(a3, tob[ti % 3])
                ti += 1

            for r in range(E // 128):
                for cpart in range(2):
                    tern(wfcT_h[r * 128:(r + 1) * 128, cpart * 4096:(cpart + 1) * 4096],
                         lambda a2, tb, r=r, cpart=cpart: [p.dma(
                             "scalar",
                             tfc_h[r * 128:(r + 1) * 128, cpart * 4096:(cpart + 1) * 4096],
                             tb[:, 0:4096], waits=[a2])],
                         4096, 0)

            NRP = FF // 128

            def tern_pr(a2, tb, r):
                dk = p.dma(
                    "scalar",
                    tpr_h[:, :, r * 128:(r + 1) * 128].rearrange("c p j -> p c j"),
                    tb[:, 0:E].rearrange("p (c j) -> p c j", j=128), waits=[a2])
                for s4 in range(4):
                    p.emit("tensor", lambda e, tb=tb, s4=s4, r=r: e.matmul(
                        pb[2 + s4][0:1, 0:512], onec_bf[:],
                        tb[:, s4 * 512:(s4 + 1) * 512],
                        start=(r == 0), stop=(r == NRP - 1)),
                        waits=(([a2] if s4 == 0 else []) +
                               ([ck["c_onec_bf"]] if (r == 0 and s4 == 0) else [])),
                        inc=("pe" if s4 == 3 else None))
                return [dk, ("pe", p.cnt["pe"])]

            for r in range(NRP):
                tern(wprT_h[r * 128:(r + 1) * 128, :],
                     lambda a2, tb, r=r: tern_pr(a2, tb, r), E, 1)
            wsk = None
            for s4 in range(4):
                wsk = p.emit("vector", lambda e, s4=s4: e.tensor_copy(
                    wrow[0:1, s4 * 512:(s4 + 1) * 512], pb[2 + s4][0:1, 0:512]),
                    waits=([("pe", p.cnt["pe"])] if s4 == 0 else [wsk]), inc="dve")
            ws_done = p.dma("scalar", ws_h[:], wrow[0:1, 0:E], waits=[wsk])
            p.fence(("tensor", "vector", "scalar"))

        if upto == "D":
            p.wait("scalar", ("dout", p.cnt["dout"]))
            with nc.Block() as block:
                p.finalize(block)
            return nc

        with ExitStack() as stF:
            enF = stF.enter_context
            hv_sb = enF(nc.sbuf_tensor("hv_sb", [128, 4 * FF], BF16))
            xh_sb = enF(nc.sbuf_tensor("xh_sb", [128, 16 * 512], BF16))
            xct_sb = [enF(nc.sbuf_tensor(f"xct_sb{i}", [128, E], BF16)) for i in range(2)]
            wfc_sb = [enF(nc.sbuf_tensor(f"wfc{i}", [128, 16 * 512], BF16)) for i in range(2)]
            wpr_sb = [enF(nc.sbuf_tensor(f"wpr{i}", [128, FF], BF16)) for i in range(2)]
            out2_sb = enF(nc.sbuf_tensor("out2_sb", [128, 4 * E], BF16))
            u0_sb = enF(nc.sbuf_tensor("u0_sb", [128, E], F32))
            x2r_sb = enF(nc.sbuf_tensor("x2r_sb", [128, E], F32))
            yt_sb = enF(nc.sbuf_tensor("yt_sb", [128, E], F32))
            o2_sb = [enF(nc.sbuf_tensor(f"o2_{i}", [128, 512], BF16)) for i in range(2)]
            hacc = enF(nc.sbuf_tensor("hacc", [128, 64], F32))
            hsq2 = enF(nc.sbuf_tensor("hsq2", [128, 64], F32))
            sgx = enF(nc.sbuf_tensor("sgx", [128, 4], F32))
            sgv = enF(nc.sbuf_tensor("sgv", [128, 4], F32))
            stat4 = enF(nc.sbuf_tensor("stat4", [128, 8], F32))
            wst = enF(nc.sbuf_tensor("wst", [1, 256], F32))
            mnegb = enF(nc.sbuf_tensor("mnegb", [128, 4], BF16))
            junkF = u0_sb[:, 0:512]
            browF = yt_sb[0:1, 0:512]

            NT = T // 512
            B0 = p.boundary()
            x2r_free = B0
            u0_free = None
            xct_free = [None, None]
            wfc_free = [B0, B0]
            wpr_free = [B0, B0]
            psH_free = [None, None]
            psC_free = [None, None]
            out2_free = []
            o2_free = [None, None]
            hacc_free = None
            xh_free = None
            hv_free = None
            mm2_pe_done = None
            prevF_m1 = None
            prev_vchain = None
            y_toks = []
            xh_ready = None
            ring4 = []
            wst_free = [B0, B0]

            lnF_p1 = {}

            def lnF_chain(f, tau):
                nonlocal x2r_free, u0_free, prevF_m1, prev_vchain
                if f >= NT:
                    return
                tc = f * 4 + tau
                xk = p.dma("sync", x2r_sb[:], x2_h[tc * 128:(tc + 1) * 128, :],
                           waits=[x2r_free])
                s2 = ln_stats(x2r_sb, E, [xk, prevF_m1])
                m1 = ln_tail(mv[:, 0:1], mv[:, 1:2], eps5[:], [s2])
                u0k = p.emit("scalar", lambda e: e.activation(
                    u0_sb[:], x2r_sb[:], AF.Identity, bias=mb[:], scale=rstd[:]),
                    waits=[m1, u0_free], inc="act")
                x2r_free = u0k
                s2b = ln_stats(u0_sb, E, [u0k])
                a1 = p.emit("scalar", lambda e: e.activation(
                    tmp1[:], mv[:, 1:2], AF.Ln, bias=eps6[:], scale=1.0),
                    waits=[s2b], inc="act")
                a2 = p.emit("scalar", lambda e: e.activation(
                    stat4[:, 0:1], tmp1[:], AF.Exp, bias=0.0, scale=-0.5),
                    waits=[a1, prev_vchain], inc="act")
                sg = p.emit("vector", lambda e, tau=tau: e.tensor_scalar(
                    sgx[:, tau:tau + 1], stat4[:, 0:1], gam[:, 2:3], 128.0,
                    ALU.mult, ALU.mult), waits=[a2], inc="dve")
                prev_vchain = sg
                p1 = p.emit("vector", lambda e, tau=tau: e.tensor_scalar(
                    xct_sb[tau % 2][:], u0_sb[:], mv[:, 0:1], None, ALU.subtract),
                    waits=[sg, xct_free[tau % 2]], inc="dve")
                u0_free = p1
                prevF_m1 = p1
                lnF_p1[tau] = p1

            def lnF_tr(f, tau):
                nonlocal xh_ready
                if f >= NT:
                    return
                p1 = lnF_p1.pop(tau)
                for ic in range(16):
                    bk_ = 4 + (len(ring4) % 4)
                    trk = p.emit("tensor", lambda e, ic=ic, bk_=bk_, tau=tau: e.transpose(
                        pbh[bk_][:, 0:128], xct_sb[tau % 2][:, ic * 128:(ic + 1) * 128],
                        idb_sb[:]),
                        waits=[p1 if ic == 0 else None,
                               xh_free if ic == 0 else None,
                               ring4[-4] if len(ring4) >= 4 else None],
                        inc="pe")
                    ring4.append(p.emit(
                        "vector", lambda e, ic=ic, bk_=bk_, tau=tau: e.tensor_copy(
                            xh_sb[:, ic * 512 + tau * 128: ic * 512 + (tau + 1) * 128],
                            pbh[bk_][:, 0:128]), waits=[trk], inc="dve"))
                xct_free[tau % 2] = ("pe", p.cnt["pe"])
                if tau == 3:
                    xh_ready = ring4[-1]

            lnF_chain(0, 0)
            lnF_chain(0, 1)
            lnF_tr(0, 0)
            lnF_chain(0, 2)
            lnF_tr(0, 1)
            lnF_chain(0, 3)
            lnF_tr(0, 2)
            lnF_tr(0, 3)
            wfc_pre = {}
            brow_cp = None
            for f in range(NT):
                hv_cp = []
                pend_tr = [None]

                def flush_tr():
                    if pend_tr[0] is None:
                        return
                    ob_, tau_, sq_ = pend_tr[0]
                    pend_tr[0] = None
                    for j in range(4):
                        bcol = tau_ * FF + (ob_ * 4 + j) * 128
                        bk_ = 4 + (len(ring4) % 4)
                        trk = p.emit("tensor", lambda e, bcol=bcol, bk_=bk_: e.transpose(
                            pbh[bk_][:, 0:128], hv_sb[:, bcol:bcol + 128], idb_sb[:]),
                            waits=[sq_ if j == 0 else None,
                                   ring4[-4] if len(ring4) >= 4 else None],
                            inc="pe")
                        tok = p.emit("vector", lambda e, bcol=bcol, bk_=bk_: e.tensor_copy(
                            hv_sb[:, bcol:bcol + 128], pbh[bk_][:, 0:128]),
                            waits=[trk], inc="dve")
                        ring4.append(tok)
                        hv_cp.append(tok)

                for ob in range(16):
                    sl = ob % 2
                    if ob in wfc_pre:
                        wk = wfc_pre.pop(ob)
                    else:
                        wk = p.dma("sync", wfc_sb[sl][:].rearrange("p (c j) -> p c j", j=512),
                                   tfc_h.rearrange("(c p) o -> p c o", p=128)[:, :, ob * 512:(ob + 1) * 512],
                                   waits=[wfc_free[sl]])
                    for tau in range(4):
                        hsl = (ob * 4 + tau) % 2
                        for ic in range(16):
                            p.emit("tensor", lambda e, sl=sl, hsl=hsl, tau=tau, ic=ic: e.matmul(
                                pb[hsl][:], xh_sb[:, ic * 512 + tau * 128: ic * 512 + (tau + 1) * 128],
                                wfc_sb[sl][:, ic * 512:(ic + 1) * 512],
                                start=(ic == 0), stop=(ic == 15)),
                                waits=([wk, xh_ready, psH_free[hsl]] if ic == 0 else []),
                                inc=("pe" if ic == 15 else None))
                        mm = ("pe", p.cnt["pe"])
                        flush_tr()
                        sli = hv_sb[:, tau * FF + ob * 512: tau * FF + (ob + 1) * 512]
                        gk = p.emit("scalar", lambda e, hsl=hsl, tau=tau, ob=ob, sli=sli: e.activation(
                            sli, pb[hsl][:], AF.Gelu, bias=0.0, scale=sgx[:, tau:tau + 1],
                            accum_out=hacc[:, ob * 4 + tau: ob * 4 + tau + 1]),
                            waits=[mm,
                                   hacc_free if (ob == 0 and tau == 0) else None,
                                   hv_free if (ob == 0 and tau == 0) else None],
                            inc="act")
                        psH_free[hsl] = gk
                        sq = p.emit("scalar", lambda e, sli=sli, ob=ob, tau=tau: e.activation(
                            junkF, sli, AF.Square, bias=0.0, scale=1.0,
                            accum_out=hsq2[:, ob * 4 + tau: ob * 4 + tau + 1]),
                            waits=[], inc="act")
                        pend_tr[0] = (ob, tau, sq)
                    wfc_free[sl] = ("act", p.cnt["act"])
                flush_tr()
                xh_free = ("pe", p.cnt["pe"])
                mm1_act = ("act", p.cnt["act"])

                chain_tok = []
                for tau in range(4):
                    hm = p.emit("vector", lambda e, tau=tau: e.tensor_reduce(
                        mv[:, 0:1], hacc[:, tau:64:4], AX.X, ALU.add),
                        waits=([mm1_act] if tau == 0 else []) + [prevF_m1], inc="dve")
                    s1 = p.emit("vector", lambda e, tau=tau: e.tensor_reduce(
                        tmp2[:], hsq2[:, tau:64:4], AX.X, ALU.add), waits=[hm], inc="dve")
                    k1 = p.emit("vector", lambda e: e.tensor_scalar(
                        mv[:, 0:1], mv[:, 0:1], 1.0 / FF, None, ALU.mult), waits=[s1], inc="dve")
                    k2 = p.emit("vector", lambda e: e.tensor_scalar(
                        tmp2[:], tmp2[:], 1.0 / FF, None, ALU.mult), waits=[k1], inc="dve")
                    k3 = p.emit("vector", lambda e: e.tensor_scalar(
                        mv[:, 1:2], mv[:, 0:1], mv[:, 0:1], -1.0, ALU.mult, ALU.mult),
                        waits=[k2], inc="dve")
                    k4 = p.emit("vector", lambda e: e.tensor_tensor(
                        mv[:, 1:2], mv[:, 1:2], tmp2[:], ALU.add), waits=[k3], inc="dve")
                    a1 = p.emit("scalar", lambda e: e.activation(
                        tmp1[:], mv[:, 1:2], AF.Ln, bias=eps6[:], scale=1.0),
                        waits=[k4], inc="act")
                    a2 = p.emit("scalar", lambda e: e.activation(
                        stat4[:, 0:1], tmp1[:], AF.Exp, bias=0.0, scale=-0.5),
                        waits=[a1, prev_vchain], inc="act")
                    sgvk = p.emit("vector", lambda e, tau=tau: e.tensor_scalar(
                        sgv[:, tau:tau + 1], stat4[:, 0:1], gam[:, 3:4], 128.0,
                        ALU.mult, ALU.mult), waits=[a2], inc="dve")
                    prev_vchain = sgvk
                    mneg = p.emit("vector", lambda e, tau=tau: e.tensor_scalar(
                        mnegb[:, tau:tau + 1], mv[:, 0:1], -1.0, None, ALU.mult),
                        waits=[k4], inc="dve")
                    prevF_m1 = mneg
                    chain_tok.append(mneg)
                hacc_free = ("dve", p.cnt["dve"])

                brow3 = browF.rearrange("p (t o) -> p t o", t=4)
                for cc in range(16):
                    sl = cc % 2
                    if cc in (1, 4, 8, 12):
                        lnF_chain(f + 1, (cc - 1) // 4 if cc == 1 else cc // 4)
                    if cc in (2, 6, 10, 14):
                        lnF_tr(f + 1, (cc - 2) // 4)
                    if cc in (3, 9) and f + 1 < NT:
                        ob_p = (cc - 3) // 6
                        wfc_pre[ob_p] = p.dma(
                            "sync", wfc_sb[ob_p][:].rearrange("p (c j) -> p c j", j=512),
                            tfc_h.rearrange("(c p) o -> p c o", p=128)[:, :, ob_p * 512:(ob_p + 1) * 512],
                            waits=[wfc_free[ob_p]])
                    wk = p.dma("sync", wpr_sb[sl][:], tpr_h[cc], waits=[wpr_free[sl]])
                    wsk = p.dma("sync", wst[0:1, sl * 128:(sl + 1) * 128],
                                ws_h[0:1, cc * 128:(cc + 1) * 128],
                                waits=[wst_free[sl],
                                       ws_done if (f == 0 and cc < 2) else None])
                    hv3 = hv_sb[:].rearrange("p (t o) -> p t o", t=4)
                    for oc in range(64):
                        w_ = [wk, psC_free[sl]] if oc == 0 else []
                        if cc == 0 and oc % 4 == 0:
                            w_ = w_ + [hv_cp[(oc // 4) * 16 + 15]]
                        p.emit("tensor", lambda e, sl=sl, cc=cc, oc=oc, hv3=hv3: e.matmul(
                            pb[2 + sl][:].rearrange("p (t o) -> p t o", t=4),
                            wpr_sb[sl][:, oc * 128:(oc + 1) * 128],
                            hv3[:, :, oc * 128:(oc + 1) * 128],
                            start=(oc == 0), stop=False),
                            waits=w_, inc=None)
                    if cc == 0:
                        for tau in range(4):
                            p.emit("tensor", lambda e, tau=tau: e.matmul(
                                pb[0][0:1, tau * 128:(tau + 1) * 128],
                                mnegb[:, tau:tau + 1], idb_sb[:],
                                start=True, stop=True),
                                waits=[chain_tok[tau]], inc=("pe" if tau == 3 else None))
                        brow_cp = p.emit("vector", lambda e: e.tensor_copy(
                            browF, pb[0][0:1, 0:512]),
                            waits=[("pe", p.cnt["pe"]),
                                   y_toks[-1] if y_toks else None], inc="dve")
                        psH_free[0] = brow_cp
                    k1m = p.emit("tensor", lambda e, sl=sl, cc=cc: e.matmul(
                        pb[2 + sl][:].rearrange("p (t o) -> p t o", t=4),
                        wst[0:1, sl * 128:(sl + 1) * 128], brow3,
                        start=False, stop=True),
                        waits=[brow_cp, wsk], inc="pe")
                    mm = ("pe", p.cnt["pe"])
                    wpr_free[sl] = mm
                    wst_free[sl] = mm
                    ev = p.emit("vector", lambda e, sl=sl: e.tensor_copy(
                        o2_sb[sl][:], pb[2 + sl][:]), waits=[mm, o2_free[sl]], inc="dve")
                    psC_free[sl] = ev
                    for tau in range(4):
                        bk_ = 4 + (len(ring4) % 4)
                        trk = p.emit("tensor", lambda e, sl=sl, tau=tau, bk_=bk_: e.transpose(
                            pbh[bk_][:, 0:128], o2_sb[sl][:, tau * 128:(tau + 1) * 128],
                            idb_sb[:]),
                            waits=[ev if tau == 0 else None,
                                   ring4[-4] if len(ring4) >= 4 else None],
                            inc="pe")
                        cpk = p.emit("vector", lambda e, cc=cc, tau=tau, bk_=bk_: e.tensor_scalar(
                            out2_sb[:, tau * E + cc * 128: tau * E + (cc + 1) * 128],
                            pbh[bk_][:, 0:128], sgv[:, tau:tau + 1], None, ALU.mult),
                            waits=[trk] + (out2_free if (cc == 0 and tau == 0) else []),
                            inc="dve")
                        ring4.append(cpk)
                    o2_free[sl] = ("pe", p.cnt["pe"])
                hv_free = ("pe", p.cnt["pe"])
                mm2_pe_done = hv_free
                out2_ready = ring4[-1]

                out2_free = []
                fin = None
                for tau in range(4):
                    tc = f * 4 + tau
                    xk = p.dma("sync", x2r_sb[:], x2_h[tc * 128:(tc + 1) * 128, :],
                               waits=[x2r_free])
                    s2 = ln_stats(out2_sb[:, tau * E:(tau + 1) * E], E,
                                  [out2_ready, prevF_m1])
                    m1 = ln_tail(mv[:, 0:1], mv[:, 1:2], eps5[:], [s2])
                    prevF_m1 = m1
                    ap1 = p.emit("scalar", lambda e, tau=tau: e.activation(
                        yt_sb[:], out2_sb[:, tau * E:(tau + 1) * E], AF.Identity,
                        bias=mb[:], scale=rstd[:]),
                        waits=[m1, fin], inc="act")
                    out2_free.append(ap1)
                    ad = p.emit("vector", lambda e: e.tensor_tensor(
                        yt_sb[:], yt_sb[:], x2r_sb[:], ALU.add), waits=[ap1, xk], inc="dve")
                    x2r_free = ad
                    dk = p.dma("scalar", y_h[tc * 128:(tc + 1) * 128, :], yt_sb[:], waits=[ad])
                    fin = dk
                    y_toks.append(dk)

            p.wait("scalar", y_toks[-1])

        with nc.Block() as block:
            p.finalize(block)
    return nc


def prep_inputs(L0_attn_x, L0_attn_w_attn, L0_attn_b_attn, L0_attn_w_proj,
                L0_attn_b_proj, L0_ffn_w_fc, L0_ffn_w_proj, T=2048):
    f32 = np.float32
    w_attn = np.ascontiguousarray(np.asarray(L0_attn_w_attn, f32))
    b_attn = np.ascontiguousarray(np.asarray(L0_attn_b_attn, f32)).reshape(1, 3 * E)
    w_proj = np.ascontiguousarray(np.asarray(L0_attn_w_proj, f32))
    b_proj = np.ascontiguousarray(np.asarray(L0_attn_b_proj, f32)).reshape(1, E)
    wfcT = np.ascontiguousarray(np.asarray(L0_ffn_w_fc, f32).T)
    wprT = np.ascontiguousarray(np.asarray(L0_ffn_w_proj, f32).T)
    x = np.asarray(L0_attn_x, f32)
    in_maps = []
    for c in range(x.shape[0]):
        in_maps.append({
            "x": np.ascontiguousarray(x[c, :T, :]),
            "w_attn": w_attn, "b_attn": b_attn,
            "w_proj": w_proj, "b_proj": b_proj,
            "wfcT": wfcT, "wprT": wprT,
        })
    return in_maps


def kernel(L0_attn_x, L0_attn_w_attn, L0_attn_b_attn, L0_attn_w_proj,
           L0_attn_b_proj, L0_ffn_w_fc, L0_ffn_w_proj):
    from concourse.bass_utils import run_bass_kernel_spmd
    nc = build(T=2048)
    in_maps = prep_inputs(L0_attn_x, L0_attn_w_attn, L0_attn_b_attn,
                          L0_attn_w_proj, L0_attn_b_proj,
                          L0_ffn_w_fc, L0_ffn_w_proj, T=2048)
    res = run_bass_kernel_spmd(nc, in_maps, core_ids=list(range(8)))
    return np.stack([res.results[c]["y"] for c in range(8)], axis=0)


# version 28
# speedup vs baseline: 1.0201x; 1.0201x over previous
import numpy as np
import ml_dtypes
import concourse.bass as bass
import concourse.mybir as mybir
from contextlib import ExitStack

F32 = mybir.dt.float32
BF16 = mybir.dt.bfloat16
AF = mybir.ActivationFunctionType
ALU = mybir.AluOpType
AX = mybir.AxisListType

E = 2048
H = 16
FF = 8192
NEG = -1e30
ESCALE = float(1.0 / np.sqrt(2048.0))
MARGIN = -20.0
CLIP_HI = float(128.0 - 1e-6)
CLIP_LO = float(-128.0 + 1e-6)


class Prog:

    ENGINES = ("sync", "gpsimd", "scalar", "vector", "tensor")

    def __init__(self, nc, stack):
        self.nc = nc
        self.q = {e: [] for e in self.ENGINES}
        self.sems = {}
        self.cnt = {}
        self.dirty = {e: False for e in self.ENGINES}
        for name in ("din", "dgp", "dout", "pe", "dve", "act", "pool"):
            self.sems[name] = stack.enter_context(nc.semaphore(name))
            self.cnt[name] = 0

    def emit(self, eng, fn, waits=(), inc=None, k=1):
        assert fn is not None or inc is None, "nop-inc is not completion-safe"
        if fn is not None:
            self.dirty[eng] = (inc is None)
        tok = None
        if inc is not None:
            self.cnt[inc] += k
            tok = (inc, self.cnt[inc])
        best = {}
        for w in waits:
            if w is None:
                continue
            grp = w if (isinstance(w, tuple) and w and isinstance(w[0], tuple)) else (w,)
            for x in grp:
                if x is None:
                    continue
                sname, v = x
                if v > best.get(sname, -1):
                    best[sname] = v
        waits = tuple(best.items())
        sems = self.sems

        def closure(e, waits=waits, fn=fn, inc=inc, k=k):
            for s, v in waits:
                e.wait_ge(sems[s], v)
            ins = fn(e) if fn is not None else None
            if inc is not None:
                if ins is None:
                    ins = e.nop()
                ins.then_inc(sems[inc], k)
        self.q[eng].append(closure)
        return tok

    def wait(self, eng, *toks):
        toks = [t for t in toks if t is not None]
        if toks:
            self.emit(eng, None, waits=toks)

    def dma(self, eng, out, in_, waits=(), **kw):
        sem = {"sync": "din", "gpsimd": "dgp", "scalar": "dout"}[eng]
        return self.emit(eng, lambda e: e.dma_start(out=out, in_=in_, **kw),
                         waits=waits, inc=sem, k=16)

    def boundary(self):
        return tuple((s, self.cnt[s]) for s in ("pe", "dve", "act", "din", "dgp", "dout"))

    def fence(self, engines=("tensor", "vector", "scalar")):
        m = {"tensor": "pe", "vector": "dve", "scalar": "act"}
        for e in engines:
            assert not self.dirty[e], f"fence with un-inc'd tail on {e}"
        toks = [(m[e], self.cnt[m[e]]) for e in engines]
        for e in engines:
            self.wait(e, *[t for t in toks if t[0] != m[e]])

    def finalize(self, block):
        for eng in self.ENGINES:
            closures = self.q[eng]
            fn = getattr(block, eng)

            def body(engine, closures=closures):
                for c in closures:
                    c(engine)
            fn(body)


def make_consts():
    a = np.arange(64)
    freqs = 10000.0 ** (-(2 * a) / 128.0)
    RT = np.zeros((128, H * 128), np.float32)
    for h in range(H):
        ang = h * freqs
        c, s = np.cos(ang), np.sin(ang)
        R = np.zeros((128, 128), np.float32)
        R[2 * a, 2 * a] = c
        R[2 * a, 2 * a + 1] = -s
        R[2 * a + 1, 2 * a] = s
        R[2 * a + 1, 2 * a + 1] = c
        RT[:, h * 128:(h + 1) * 128] = R.T
    stair = np.where(np.tril(np.ones((128, 128), bool)), 0.0, NEG).astype(np.float32)
    return {
        "c_rt": RT.astype(ml_dtypes.bfloat16),
        "c_idb": np.eye(128).astype(ml_dtypes.bfloat16),
        "c_idf": np.ones((1, 1), dtype=np.float32),
        "c_stair": stair.astype(ml_dtypes.bfloat16),
        "c_stairT": np.ascontiguousarray(stair.T).astype(ml_dtypes.bfloat16),
        "c_onec_bf": np.ones((128, 1), ml_dtypes.bfloat16),
        "c_oner_bf": np.ones((1, 128), ml_dtypes.bfloat16),
        "c_onec_f": np.ones((128, 1), np.float32),
        "c_oner_f": np.ones((1, 128), np.float32),
    }


def build(T=2048, upto="F", debug=False):
    NS = T // 512
    NTC = T // 128
    nc = bass.Bass("TRN2", target_bir_lowering=False)

    x_h = nc.dram_tensor("x", [T, E], F32, kind="ExternalInput").ap()
    wattn_h = nc.dram_tensor("w_attn", [E, 3 * E], F32, kind="ExternalInput").ap()
    battn_h = nc.dram_tensor("b_attn", [1, 3 * E], F32, kind="ExternalInput").ap()
    wproj_h = nc.dram_tensor("w_proj", [E, E], F32, kind="ExternalInput").ap()
    bproj_h = nc.dram_tensor("b_proj", [1, E], F32, kind="ExternalInput").ap()
    wfcT_h = nc.dram_tensor("wfcT", [E, FF], F32, kind="ExternalInput").ap()
    wprT_h = nc.dram_tensor("wprT", [FF, E], F32, kind="ExternalInput").ap()
    y_h = nc.dram_tensor("y", [T, E], F32, kind="ExternalOutput").ap()

    sk = "ExternalOutput" if debug else "Internal"
    tfc_h = nc.dram_tensor("s_tfc", [E, FF], BF16, kind=sk).ap()
    tpr_h = nc.dram_tensor("s_tpr", [16, 128, FF], BF16, kind=sk).ap()
    o_h = nc.dram_tensor("s_o", [T // 128, 128, E], BF16, kind=sk).ap()
    x2_h = nc.dram_tensor("s_x2", [T, E], F32, kind=sk).ap()
    ws_h = nc.dram_tensor("s_ws", [1, E], F32, kind=sk).ap()

    cap = {k: nc.inline_tensor(v, k).ap() for k, v in make_consts().items()}

    st = ExitStack()
    with st:
        en = st.enter_context
        p = Prog(nc, st)

        pb = [en(nc.psum_tensor(f"pb{i}", [128, 512], F32)) for i in range(8)]
        pbh = [b[:].bitcast(BF16) for b in pb]

        rt_sb = en(nc.sbuf_tensor("rt_sb", [128, H * 128], BF16))
        idb_sb = en(nc.sbuf_tensor("idb_sb", [128, 128], BF16))
        idf_sb = en(nc.sbuf_tensor("idf_sb", [1, 1], F32))
        stair_sb = en(nc.sbuf_tensor("stair_sb", [128, 128], BF16))
        stairT_sb = en(nc.sbuf_tensor("stairT_sb", [128, 128], BF16))
        onec_bf = en(nc.sbuf_tensor("onec_bf", [128, 1], BF16))
        oner_bf = en(nc.sbuf_tensor("oner_bf", [1, 128], BF16))
        onec_f = en(nc.sbuf_tensor("onec_f", [128, 1], F32))
        oner_f = en(nc.sbuf_tensor("oner_f", [1, 128], F32))
        bcols = en(nc.sbuf_tensor("bcols", [128, 32], F32))
        bvrow = en(nc.sbuf_tensor("bvrow", [1, E], BF16))
        eps5 = en(nc.sbuf_tensor("eps5", [128, 1], F32))
        eps6 = en(nc.sbuf_tensor("eps6", [128, 1], F32))
        st6 = en(nc.sbuf_tensor("st6", [128, 24], F32))
        mv = en(nc.sbuf_tensor("mv", [128, 2], F32))
        rstd = en(nc.sbuf_tensor("rstd", [128, 1], F32))
        mb = en(nc.sbuf_tensor("mb", [128, 1], F32))
        tmp1 = en(nc.sbuf_tensor("tmp1", [128, 1], F32))
        tmp2 = en(nc.sbuf_tensor("tmp2", [128, 1], F32))
        gam = en(nc.sbuf_tensor("gam", [128, 4], F32))
        nth = en(nc.sbuf_tensor("nth", [128, 2], F32))
        gsc = en(nc.sbuf_tensor("gsc", [1, 2], F32))

        ck = {}
        for dst, key in [(rt_sb, "c_rt"), (idb_sb, "c_idb"), (idf_sb, "c_idf"),
                         (stair_sb, "c_stair"), (stairT_sb, "c_stairT"),
                         (onec_bf, "c_onec_bf"), (oner_bf, "c_oner_bf"),
                         (onec_f, "c_onec_f"), (oner_f, "c_oner_f")]:
            ck[key] = p.dma("sync", dst[:], cap[key])
        p.emit("vector", lambda e: e.memset(eps5[:], 1e-5), inc="dve")
        p.emit("vector", lambda e: e.memset(eps6[:], 1e-6), inc="dve")

        def ln_tail(mean_ap, var_ap, eps_ap, waits):
            a1 = p.emit("scalar", lambda e: e.activation(
                tmp1[:], var_ap, AF.Ln, bias=eps_ap, scale=1.0), waits=waits, inc="act")
            a2 = p.emit("scalar", lambda e: e.activation(
                rstd[:], tmp1[:], AF.Exp, bias=0.0, scale=-0.5), waits=[a1], inc="act")
            return p.emit("vector", lambda e: e.tensor_scalar(
                mb[:], mean_ap, rstd[:], -1.0, ALU.mult, ALU.mult), waits=[a2], inc="dve")

        def ln_stats(src_tile, W, waits):
            nblk = W // 512
            s1 = None
            for i in range(nblk):
                s1 = p.emit("vector", lambda e, i=i: e.bn_stats(
                    st6[:, i * 6:(i + 1) * 6], src_tile[:, i * 512:(i + 1) * 512]),
                    waits=(list(waits) if i == 0 else [s1]), inc="dve")
            return p.emit("vector", lambda e, nblk=nblk: e.bn_aggr(
                mv[:], st6[:, 0:6 * nblk]), waits=[s1], inc="dve")

        with ExitStack() as stT:
            enT = stT.enter_context
            brow = enT(nc.sbuf_tensor("brow", [1, 3 * E], F32))
            tw = [enT(nc.sbuf_tensor(f"tw{i}", [128, 4096], F32)) for i in range(4)]
            ta = [enT(nc.sbuf_tensor(f"ta{i}", [128, 4096], BF16)) for i in range(2)]
            tb = [enT(nc.sbuf_tensor(f"tb{i}", [128, 4096], BF16)) for i in range(2)]
            tob = [enT(nc.sbuf_tensor(f"tob{i}", [128, 4096], BF16)) for i in range(2)]
            acc = enT(nc.sbuf_tensor("acc", [128, 2], F32))

            t_b = p.dma("sync", brow[:], battn_h)
            prev_copy = None
            for j in range(32):
                tk = p.emit("tensor", lambda e, j=j: e.transpose(
                    pb[0][:, 0:1], brow[0:1, j * 128:(j + 1) * 128], idf_sb[:]),
                    waits=([t_b, ck["c_idf"]] if j == 0 else []) + [prev_copy],
                    inc="pe")
                prev_copy = p.emit("vector", lambda e, j=j: e.tensor_copy(
                    bcols[:, j:j + 1], pb[0][:, 0:1]), waits=[tk], inc="dve")
            bv = p.emit("vector", lambda e: e.tensor_copy(bvrow[:], brow[0:1, 2 * E:3 * E]),
                        waits=[t_b, prev_copy], inc="dve")

            z0 = p.emit("vector", lambda e: e.memset(acc[:], 0.0), waits=[bv], inc="dve")
            dve_last = z0
            ti = 0
            tw_free = [[None]] * 4
            red2 = enT(nc.sbuf_tensor("red2", [128, 2], F32))

            def absum(src_ap, width, col):
                nonlocal dve_last, ti
                buf = tw[ti % 4]
                tk = p.dma("sync", buf[:, 0:width], src_ap, waits=tw_free[ti % 4])
                if ti % 2 == 0:
                    rk = p.emit("scalar", lambda e, buf=buf, width=width, ti=ti: e.activation(
                        ta[ti % 2][:, 0:width], buf[:, 0:width], AF.Abs, bias=0.0, scale=1.0,
                        accum_out=red2[:, ti % 2:ti % 2 + 1]),
                        waits=[tk, dve_last if ti >= 2 else None], inc="act")
                else:
                    rk = p.emit("vector", lambda e, buf=buf, width=width, ti=ti: e.tensor_reduce(
                        red2[:, ti % 2:ti % 2 + 1], buf[:, 0:width], AX.X, ALU.add,
                        apply_absolute_value=True),
                        waits=[tk, dve_last if ti >= 2 else None], inc="dve")
                tw_free[ti % 4] = [rk]
                dve_last = p.emit("vector", lambda e, col=col, ti=ti: e.tensor_tensor(
                    acc[:, col:col + 1], acc[:, col:col + 1], red2[:, ti % 2:ti % 2 + 1],
                    ALU.add), waits=[rk], inc="dve")
                ti += 1

            for r in range(E // 128):
                for cpart in range(2):
                    absum(wfcT_h[r * 128:(r + 1) * 128, cpart * 4096:(cpart + 1) * 4096], 4096, 0)
            for r in range(FF // 128):
                absum(wprT_h[r * 128:(r + 1) * 128, :], E, 1)

            p.emit("tensor", lambda e: e.matmul(pb[0][0:1, 0:1], acc[:, 0:1],
                                                onec_f[:], start=True, stop=True),
                   waits=[dve_last, ck["c_onec_f"], prev_copy], inc=None)
            tk = p.emit("tensor", lambda e: e.matmul(pb[0][0:1, 128:129], acc[:, 1:2],
                                                     onec_f[:], start=True, stop=True),
                        inc="pe")
            g1a = p.emit("vector", lambda e: e.tensor_copy(
                gsc[0:1, 0:1], pb[0][0:1, 0:1]), waits=[tk], inc="dve")
            g1 = p.emit("vector", lambda e: e.tensor_copy(
                gsc[0:1, 1:2], pb[0][0:1, 128:129]), waits=[g1a], inc="dve")
            g2 = p.emit("vector", lambda e: e.tensor_scalar(
                gsc[0:1, 0:2], gsc[0:1, 0:2], 1.0 / (E * FF), None, ALU.mult),
                waits=[g1], inc="dve")
            bk = p.emit("tensor", lambda e: e.matmul(pb[1][:, 0:2], oner_f[0:1, :],
                                                     gsc[0:1, 0:2], start=True, stop=True),
                        waits=[g2, ck["c_oner_f"]], inc="pe")
            g3 = p.emit("vector", lambda e: e.tensor_copy(gam[:, 0:2], pb[1][:, 0:2]),
                        waits=[bk], inc="dve")
            g4 = p.emit("vector", lambda e: e.tensor_scalar(
                gam[:, 2:4], gam[:, 0:2], 1.0 / 256.0, None, ALU.mult), waits=[g3], inc="dve")
            g5 = p.emit("vector", lambda e: e.tensor_scalar(
                gam[:, 0:2], gam[:, 0:2], 1e-6, 0.5, ALU.add, ALU.mult), waits=[g4], inc="dve")
            g6 = p.emit("vector", lambda e: e.tensor_scalar(
                nth[:, 0:2], gam[:, 0:2], -1.0, None, ALU.mult), waits=[g5], inc="dve")

            BIG = 1e30
            ta_free = [None, None]
            tb_free = [None, None]
            tob_free = [[None], [None]]

            def tern(src_ap, dst_fn, width, col):
                nonlocal ti
                buf = tw[ti % 4]
                tk = p.dma("sync", buf[:, 0:width], src_ap, waits=tw_free[ti % 4])
                a1 = p.emit("scalar", lambda e, buf=buf, width=width, col=col, ti=ti: e.activation(
                    ta[ti % 2][:, 0:width], buf[:, 0:width], AF.Sign,
                    bias=nth[:, col:col + 1], scale=1.0),
                    waits=[tk, g6, ta_free[ti % 2]], inc="act")
                if ti % 4 == 0:
                    b2 = p.emit("scalar", lambda e, buf=buf, width=width, col=col, ti=ti: e.activation(
                        tb[ti % 2][:, 0:width], buf[:, 0:width], AF.Sign,
                        bias=gam[:, col:col + 1], scale=1.0),
                        waits=[tb_free[ti % 2]], inc="act")
                    tw_free[ti % 4] = [b2]
                else:
                    b1 = p.emit("vector", lambda e, buf=buf, width=width, col=col, ti=ti: e.tensor_scalar(
                        tb[ti % 2][:, 0:width], buf[:, 0:width], gam[:, col:col + 1], BIG,
                        ALU.add, ALU.mult),
                        waits=[tk, g6, tb_free[ti % 2]], inc="dve")
                    b2 = p.emit("vector", lambda e, width=width, ti=ti: e.tensor_scalar(
                        tb[ti % 2][:, 0:width], tb[ti % 2][:, 0:width], 1.0, -1.0,
                        ALU.min, ALU.max), waits=[b1], inc="dve")
                    tw_free[ti % 4] = [a1, b1]
                a3 = p.emit("vector", lambda e, width=width, ti=ti: e.tensor_tensor(
                    tob[ti % 2][:, 0:width], ta[ti % 2][:, 0:width],
                    tb[ti % 2][:, 0:width], ALU.add),
                    waits=[a1, b2] + tob_free[ti % 2], inc="dve")
                ta_free[ti % 2] = a3
                tb_free[ti % 2] = a3
                tob_free[ti % 2] = dst_fn(a3, tob[ti % 2])
                ti += 1

            for r in range(E // 128):
                for cpart in range(2):
                    tern(wfcT_h[r * 128:(r + 1) * 128, cpart * 4096:(cpart + 1) * 4096],
                         lambda a2, tb, r=r, cpart=cpart: [p.dma(
                             "scalar",
                             tfc_h[r * 128:(r + 1) * 128, cpart * 4096:(cpart + 1) * 4096],
                             tb[:, 0:4096], waits=[a2])],
                         4096, 0)

            NRP = FF // 128

            def tern_pr(a2, tb, r):
                dk = p.dma(
                    "scalar",
                    tpr_h[:, :, r * 128:(r + 1) * 128].rearrange("c p j -> p c j"),
                    tb[:, 0:E].rearrange("p (c j) -> p c j", j=128), waits=[a2])
                for s4 in range(4):
                    p.emit("tensor", lambda e, tb=tb, s4=s4, r=r: e.matmul(
                        pb[2 + s4][0:1, 0:512], onec_bf[:],
                        tb[:, s4 * 512:(s4 + 1) * 512],
                        start=(r == 0), stop=(r == NRP - 1)),
                        waits=(([a2] if s4 == 0 else []) +
                               ([ck["c_onec_bf"]] if (r == 0 and s4 == 0) else [])),
                        inc=("pe" if s4 == 3 else None))
                return [dk, ("pe", p.cnt["pe"])]

            for r in range(NRP):
                tern(wprT_h[r * 128:(r + 1) * 128, :],
                     lambda a2, tb, r=r: tern_pr(a2, tb, r), E, 1)
            wsk = None
            for s4 in range(4):
                wsk = p.emit("vector", lambda e, s4=s4: e.tensor_copy(
                    brow[0:1, s4 * 512:(s4 + 1) * 512], pb[2 + s4][0:1, 0:512]),
                    waits=([("pe", p.cnt["pe"])] if s4 == 0 else [wsk]), inc="dve")
            ws_done = p.dma("scalar", ws_h[:], brow[0:1, 0:E], waits=[wsk])
            p.fence(("tensor", "vector", "scalar"))
        if upto == "T":
            with nc.Block() as block:
                p.finalize(block)
            return nc

        with ExitStack() as stA:
            enA = stA.enter_context
            K_sb = enA(nc.sbuf_tensor("K_sb", [128, H * T], BF16))
            V_sb = enA(nc.sbuf_tensor("V_sb", [128, NTC * E], BF16))
            Q_sb = enA(nc.sbuf_tensor("Q_sb", [128, H * 512], BF16))
            xn_sb = enA(nc.sbuf_tensor("xn_sb", [128, 16 * 512], BF16))
            spareA = enA(nc.sbuf_tensor("spareA", [128, 128], BF16))
            xt_sb = enA(nc.sbuf_tensor("xt_sb", [128, E], F32))
            wblk = [enA(nc.sbuf_tensor(f"wblk{i}", [128, 16 * 256], BF16)) for i in range(2)]
            E_sb = [enA(nc.sbuf_tensor(f"E_sb{i}", [128, 512], BF16)) for i in range(2)]
            o_sb = [enA(nc.sbuf_tensor(f"o_sb{i}", [128, 512], BF16)) for i in range(4)]
            sbc_sb = [enA(nc.sbuf_tensor(f"sbc_sb{i}", [128, 512], BF16)) for i in range(2)]
            negm4 = enA(nc.sbuf_tensor("negm4", [128, 4], BF16))
            negrow = [enA(nc.sbuf_tensor(f"negrow{i}", [1, 512], BF16)) for i in range(2)]
            sinvb = [enA(nc.sbuf_tensor(f"sinvb{i}", [1, 512], BF16)) for i in range(2)]

            B0 = p.boundary()
            xt_free = B0
            prev_m1 = None
            wblk_free = [B0, B0]
            o_free = [None, None, None, None]
            ln_apply_tok = [None] * 4
            xn_read_done = B0

            def ln_block_tau(s, tau):
                nonlocal xt_free, prev_m1, xn_read_done
                tc = s * 4 + tau
                xk = p.dma("sync", xt_sb[:], x_h[tc * 128:(tc + 1) * 128, :],
                           waits=[xt_free])
                s2 = ln_stats(xt_sb, E, [xk, prev_m1])
                m1 = ln_tail(mv[:, 0:1], mv[:, 1:2], eps5[:], [s2])
                prev_m1 = m1
                ak = p.emit("scalar", lambda e, tau=tau: e.activation(
                    xn_sb[:, tau * 2048:(tau + 1) * 2048], xt_sb[:],
                    AF.Identity, bias=mb[:], scale=rstd[:]),
                    waits=[m1, xn_read_done if tau == 0 else None], inc="act")
                ln_apply_tok[tau] = ak
                xt_free = ak

            NB_A = 64
            dstA = [(b % 16) * 4 + b // 16 for b in range(NB_A)]
            cycA, fixA = [], []
            seenA = [False] * NB_A
            for b0 in range(NB_A):
                if seenA[b0]:
                    continue
                if dstA[b0] == b0:
                    fixA.append(b0); seenA[b0] = True; continue
                cyc = [b0]; seenA[b0] = True; nxt = dstA[b0]
                while nxt != b0:
                    cyc.append(nxt); seenA[nxt] = True; nxt = dstA[nxt]
                cycA.append(cyc)

            for tau in range(4):
                ln_block_tau(0, tau)

            for s in range(NS):
                axn_copies = []

                def amove(srcb, dstb, extra=()):
                    bk_ = 2 + (len(axn_copies) % 4)
                    trk = p.emit("tensor", lambda e, srcb=srcb, bk_=bk_: e.transpose(
                        pbh[bk_][:, 0:128], xn_sb[:, srcb * 128:(srcb + 1) * 128],
                        idb_sb[:]),
                        waits=list(extra) +
                              ([axn_copies[-4]] if len(axn_copies) >= 4 else []),
                        inc="pe")
                    axn_copies.append(p.emit(
                        "vector", lambda e, dstb=dstb, bk_=bk_: e.tensor_copy(
                            xn_sb[:, dstb * 128:(dstb + 1) * 128], pbh[bk_][:, 0:128]),
                        waits=[trk], inc="dve"))

                first_wA = list(ln_apply_tok) + [ck["c_idb"]]
                for b0 in fixA:
                    amove(b0, b0, first_wA if not axn_copies else ())
                for cyc in cycA:
                    blast = cyc[-1]
                    bk_ = 2 + (len(axn_copies) % 4)
                    trk = p.emit("tensor", lambda e, blast=blast, bk_=bk_: e.transpose(
                        pbh[bk_][:, 0:128], xn_sb[:, blast * 128:(blast + 1) * 128],
                        idb_sb[:]),
                        waits=(first_wA if not axn_copies else []) +
                              ([axn_copies[-4]] if len(axn_copies) >= 4 else []),
                        inc="pe")
                    sv = p.emit("vector", lambda e, bk_=bk_: e.tensor_copy(
                        spareA[:], pbh[bk_][:, 0:128]), waits=[trk], inc="dve")
                    axn_copies.append(sv)
                    for i in range(len(cyc) - 1, 0, -1):
                        amove(cyc[i - 1], cyc[i])
                    axn_copies.append(p.emit(
                        "vector", lambda e, b0=cyc[0]: e.tensor_copy(
                            xn_sb[:, b0 * 128:(b0 + 1) * 128], spareA[:]),
                        waits=[axn_copies[-1]], inc="dve"))
                xn_ready = axn_copies[-1]
                if upto == "Aln":
                    break

                psq_free = [axn_copies[-2], axn_copies[-1]]
                rope_free = [axn_copies[-4], axn_copies[-3]]
                pend_rope = [None, None]
                esb_free = [None, None]
                psv_free = [None, None, None, None]
                for b in range(24):
                    wk = p.dma("gpsimd", wblk[b % 2][:].rearrange("p (c j) -> p c j", j=256),
                               wattn_h[:, b * 256:(b + 1) * 256].rearrange("(c p) j -> p c j", p=128),
                               waits=[wblk_free[b % 2]])
                    if b < 16:
                        for jj in range(2):
                            j = 2 * b + jj
                            h = j % 16
                            for ic in range(16):
                                p.emit("tensor", lambda e, b=b, jj=jj, ic=ic: e.matmul(
                                    pb[2 + jj][:], wblk[b % 2][:, ic * 256 + jj * 128: ic * 256 + (jj + 1) * 128],
                                    xn_sb[:, ic * 512:(ic + 1) * 512],
                                    start=(ic == 0), stop=(ic == 15)),
                                    waits=([wk, xn_ready, psq_free[jj]] if ic == 0 else []),
                                    inc=("pe" if ic == 15 else None))
                            mm = ("pe", p.cnt["pe"])
                            ev1 = p.emit("scalar", lambda e, jj=jj, j=j: e.activation(
                                E_sb[jj][:], pb[2 + jj][:], AF.Identity,
                                bias=bcols[:, j:j + 1], scale=1.0),
                                waits=[mm, esb_free[jj]], inc="act")
                            psq_free[jj] = ev1
                            if pend_rope[jj] is not None:
                                pend_rope[jj]()
                                pend_rope[jj] = None

                            def do_rope(jj=jj, h=h, j=j, b=b, ev1=ev1):
                                rk = p.emit("tensor", lambda e, jj=jj, h=h: e.matmul(
                                    pb[4 + jj][:], rt_sb[:, h * 128:(h + 1) * 128],
                                    E_sb[jj][:], start=True, stop=True),
                                    waits=[ev1, ck["c_rt"] if b == 0 else None,
                                           rope_free[jj]], inc="pe")
                                esb_free[jj] = ("pe", p.cnt["pe"])
                                if j < 16:
                                    dst = Q_sb[:, h * 512:(h + 1) * 512]
                                else:
                                    dst = K_sb[:, h * T + s * 512: h * T + (s + 1) * 512]
                                ev2 = p.emit("vector", lambda e, jj=jj, dst=dst: e.tensor_copy(
                                    dst, pb[4 + jj][:]), waits=[rk], inc="dve")
                                rope_free[jj] = ev2
                            pend_rope[jj] = do_rope
                    else:
                        for jj in range(2):
                            if pend_rope[jj] is not None:
                                pend_rope[jj]()
                                pend_rope[jj] = None
                        vb = b - 16
                        for tau in range(4):
                            bi = 2 + tau
                            for ic in range(16):
                                p.emit("tensor", lambda e, b=b, tau=tau, ic=ic, bi=bi: e.matmul(
                                    pb[bi][:, 0:256],
                                    xn_sb[:, ic * 512 + tau * 128: ic * 512 + (tau + 1) * 128],
                                    wblk[b % 2][:, ic * 256:(ic + 1) * 256],
                                    start=(ic == 0), stop=False),
                                    waits=([wk, psv_free[tau],
                                            psq_free[0] if b == 16 and tau == 0 else None,
                                            psq_free[1] if b == 16 and tau == 1 else None,
                                            rope_free[0] if b == 16 and tau == 2 else None,
                                            rope_free[1] if b == 16 and tau == 3 else None]
                                           if ic == 0 else []),
                                    inc=None)
                            p.emit("tensor", lambda e, bi=bi, vb=vb: e.matmul(
                                pb[bi][:, 0:256], oner_bf[0:1, :],
                                bvrow[0:1, vb * 256:(vb + 1) * 256],
                                start=False, stop=True), inc="pe")
                            mm = ("pe", p.cnt["pe"])
                            ev = p.emit("vector", lambda e, bi=bi, s=s, tau=tau, vb=vb: e.tensor_copy(
                                V_sb[:, (s * 4 + tau) * E + vb * 256:(s * 4 + tau) * E + (vb + 1) * 256],
                                pb[bi][:, 0:256]), waits=[mm], inc="dve")
                            psv_free[tau] = ev
                    wblk_free[b % 2] = ("pe", p.cnt["pe"])
                xn_read_done = ("pe", p.cnt["pe"])
                p.fence(("tensor", "vector", "scalar"))
                if debug and s == 0:
                    dbg_q = nc.dram_tensor("d_q", [128, H * 512], BF16, kind="ExternalOutput").ap()
                    dbg_k = nc.dram_tensor("d_k", [128, H * T], BF16, kind="ExternalOutput").ap()
                    dbg_v = nc.dram_tensor("d_v", [128, NTC * E], BF16, kind="ExternalOutput").ap()
                    dbg_xn = nc.dram_tensor("d_xn", [128, 16 * 512], BF16, kind="ExternalOutput").ap()
                    p.dma("scalar", dbg_q, Q_sb[:])
                    p.dma("scalar", dbg_k, K_sb[:])
                    p.dma("scalar", dbg_v, V_sb[:])
                    p.dma("scalar", dbg_xn, xn_sb[:])
                    p.wait("scalar", ("dout", p.cnt["dout"]))
                if upto in ("Aq", "Aln"):
                    break

                psmax_free = [None, None]
                red_tok = [None] * 4
                negm_free = [None] * 4
                negrow_ready = [None, None]
                negrow_free = [None, None]
                esb_free = [None, None]
                exp_tok = [None, None]
                sbc_free = [None, None]
                sum_free = [None, None]
                sinvb_free = None
                o_ps_free = [None, None]
                negtr_free = [None, None]

                def plan_a1(h):
                    for tau in range(4):
                        g = 4 * s + tau
                        Ls = 128 * min((g + 2) // 2, 4)
                        bi = tau % 2
                        p.emit("tensor", lambda e, h=h, tau=tau, Ls=Ls, bi=bi: e.matmul(
                            pb[bi][:, 0:Ls],
                            Q_sb[:, h * 512 + tau * 128: h * 512 + (tau + 1) * 128],
                            K_sb[:, h * T: h * T + Ls],
                            start=True, stop=True),
                            waits=[psmax_free[bi]], inc="pe")
                        mm = ("pe", p.cnt["pe"])
                        mk = None
                        if g == 0:
                            mk = p.emit("vector", lambda e: e.tensor_tensor(
                                pb[0][:, 0:128], pb[0][:, 0:128], stair_sb[:], ALU.add),
                                waits=[mm, ck["c_stair"]], inc="dve")
                        red_tok[tau] = p.emit("vector", lambda e, Ls=Ls, bi=bi, tau=tau: e.tensor_reduce(
                            negm4[:, tau:tau + 1], pb[bi][:, 0:Ls], AX.X, ALU.max, negate=True),
                            waits=[mm, mk, negm_free[tau]], inc="dve")
                        psmax_free[bi] = red_tok[tau]

                def plan_a2(h):
                    last = None
                    for tau in range(4):
                        bi = tau % 2
                        trk = p.emit("tensor", lambda e, tau=tau, bi=bi: e.transpose(
                            pbh[bi][0:1, 512 + tau * 128: 512 + (tau + 1) * 128],
                            negm4[:, tau:tau + 1], idb_sb[:]),
                            waits=[red_tok[tau], red_tok[bi + 2], negtr_free[bi]],
                            inc="pe")
                        negm_free[tau] = trk
                        last = p.emit("vector", lambda e, h=h, tau=tau, bi=bi: e.tensor_scalar(
                            negrow[h % 2][0:1, tau * 128:(tau + 1) * 128],
                            pbh[bi][0:1, 512 + tau * 128: 512 + (tau + 1) * 128],
                            MARGIN, None, ALU.add),
                            waits=[trk, negrow_free[h % 2] if tau == 0 else None], inc="dve")
                        negtr_free[bi] = last
                        psmax_free[bi] = last
                    negrow_ready[h % 2] = last

                pend_tail = [None]

                def plan_b(h):
                    nonlocal sinvb_free
                    nk = 4 * s + 4
                    ob_ = 4 + h % 2
                    sb_ = 6 + h % 2

                    def sum_av(kc):
                        off = max(0, (kc - 4 * s) * 128)
                        sl = kc % 2
                        p.emit("tensor", lambda e, kc=kc, off=off, sl=sl: e.matmul(
                            pb[sb_][0:1, off:512], onec_bf[:], E_sb[sl][:, off:512],
                            start=(kc == 0), stop=(kc == nk - 1)),
                            waits=[exp_tok[sl], sum_free[h % 2] if kc == 0 else None],
                            inc=None)
                        p.emit("tensor", lambda e, h=h, kc=kc, off=off, sl=sl: e.matmul(
                            pb[ob_][:, off:512], V_sb[:, kc * E + h * 128: kc * E + (h + 1) * 128],
                            E_sb[sl][:, off:512],
                            start=(kc == 0), stop=(kc == nk - 1)),
                            waits=[o_ps_free[h % 2]] if kc == 0 else [], inc="pe")
                        esb_free[sl] = ("pe", p.cnt["pe"])

                    for kc in range(nk):
                        off = max(0, (kc - 4 * s) * 128)
                        sl = kc % 2
                        p.emit("tensor", lambda e, h=h, kc=kc, off=off, sl=sl: e.matmul(
                            pb[2 + sl][:, off:512],
                            K_sb[:, h * T + kc * 128: h * T + (kc + 1) * 128],
                            Q_sb[:, h * 512 + off:(h + 1) * 512],
                            start=True, stop=False),
                            waits=[exp_tok[sl],
                                   negrow_ready[h % 2] if kc == 0 else None],
                            inc=None)
                        if kc >= 4 * s:
                            p.emit("tensor", lambda e, off=off, sl=sl: e.matmul(
                                pb[2 + sl][:, off:off + 128], stair_sb[:], idb_sb[:],
                                start=False, stop=False), inc=None)
                        nmk = p.emit("tensor", lambda e, h=h, off=off, sl=sl: e.matmul(
                            pb[2 + sl][:, off:512], oner_bf[0:1, :],
                            negrow[h % 2][0:1, off:512], start=False, stop=True),
                            inc="pe")
                        ek = p.emit("scalar", lambda e, off=off, sl=sl: e.activation(
                            E_sb[sl][:, off:512], pb[2 + sl][:, off:512], AF.Exp,
                            bias=0.0, scale=1.0), waits=[nmk, esb_free[sl]], inc="act")
                        exp_tok[sl] = ek
                        if kc == 3 and pend_tail[0] is not None:
                            pend_tail[0]()
                            pend_tail[0] = None
                        if kc >= 1:
                            sum_av(kc - 1)
                    sum_av(nk - 1)
                    mm_last = ("pe", p.cnt["pe"])
                    def tail(h=h, s=s, ob_=ob_, sb_=sb_, mm_last=mm_last):
                        nonlocal sinvb_free
                        r1 = p.emit("vector", lambda e: e.reciprocal(
                            pb[sb_][0:1, :], pb[sb_][0:1, :]), waits=[mm_last], inc="dve")
                        r2 = p.emit("vector", lambda e: e.tensor_scalar(
                            sinvb[h % 2][0:1, :], pb[sb_][0:1, :], ESCALE, None, ALU.mult),
                            waits=[r1, sinvb_free], inc="dve")
                        bk2 = p.emit("tensor", lambda e: e.matmul(
                            pb[sb_][:], oner_bf[0:1, :], sinvb[h % 2][0:1, :],
                            start=True, stop=True), waits=[r2], inc="pe")
                        sinvb_free = bk2
                        c1 = p.emit("vector", lambda e: e.tensor_copy(
                            sbc_sb[h % 2][:], pb[sb_][:]), waits=[bk2, sbc_free[h % 2]],
                            inc="dve")
                        sum_free[h % 2] = c1
                        om = p.emit("vector", lambda e: e.tensor_tensor(
                            o_sb[h % 4][:], pb[ob_][:], sbc_sb[h % 2][:], ALU.mult),
                            waits=[c1, o_free[h % 4]], inc="dve")
                        o_ps_free[h % 2] = om
                        sbc_free[h % 2] = om
                        dk = p.dma("scalar",
                                   o_h[s * 4:(s + 1) * 4, :, h * 128:(h + 1) * 128]
                                   .rearrange("c p j -> p c j"),
                                   o_sb[h % 4][:].rearrange("p (c j) -> p c j", j=128),
                                   waits=[om])
                        o_free[h % 4] = dk
                    pend_tail[0] = tail
                    negrow_free[h % 2] = ("pe", p.cnt["pe"])

                for h in range(H):
                    plan_a1(h)
                    if h > 0:
                        plan_b(h - 1)
                    plan_a2(h)
                    if s + 1 < NS and h in (4, 7, 10, 13):
                        ln_block_tau(s + 1, (h - 4) // 3)
                plan_b(H - 1)
                if pend_tail[0] is not None:
                    pend_tail[0]()
                    pend_tail[0] = None
                p.fence(("tensor", "vector", "scalar"))
        if upto in ("A", "Aq", "Aln"):
            p.wait("scalar", ("dout", p.cnt["dout"]))
            with nc.Block() as block:
                p.finalize(block)
            return nc

        with ExitStack() as stD:
            enD = stD.enter_context
            wp_sb = enD(nc.sbuf_tensor("wp_sb", [128, 16 * E], BF16))
            osb = [enD(nc.sbuf_tensor(f"osb{i}", [128, 16 * 128], BF16)) for i in range(2)]
            ao_sb = [enD(nc.sbuf_tensor(f"ao_sb{i}", [128, E], F32)) for i in range(2)]
            junk = [enD(nc.sbuf_tensor(f"junkD{i}", [128, E], F32)) for i in range(2)]
            b_pb = enD(nc.sbuf_tensor("b_pb", [128, E], F32))
            bprow = enD(nc.sbuf_tensor("bprow", [1, E], F32))
            xd_sb = enD(nc.sbuf_tensor("xd_sb", [128, E], F32))
            x2w_sb = [enD(nc.sbuf_tensor(f"x2w_sb{i}", [128, E], F32)) for i in range(2)]
            psD = [[pb[0], pb[1], pb[2], pb[3]], [pb[4], pb[5], pb[6], pb[7]]]

            B0 = p.boundary()
            bp = p.dma("sync", bprow[:], bproj_h, waits=[B0])
            bb = None
            for c4 in range(4):
                bb = p.emit("tensor", lambda e, c4=c4: e.matmul(
                    psD[0][c4][:], oner_f[0:1, :],
                    bprow[0:1, c4 * 512:(c4 + 1) * 512], start=True, stop=True),
                    waits=[bp] if c4 == 0 else [], inc="pe")
            bpc = None
            for c4 in range(4):
                bpc = p.emit("vector", lambda e, c4=c4: e.tensor_copy(
                    b_pb[:, c4 * 512:(c4 + 1) * 512], psD[0][c4][:]), waits=[bb], inc="dve")
            wpk = None
            for dc in range(16):
                wpk = p.dma("gpsimd", wp_sb[:, dc * E:(dc + 1) * E],
                            wproj_h[dc * 128:(dc + 1) * 128, :],
                            waits=[B0] if dc == 0 else [])
            osb_free = [B0, B0]
            ps_free = [bpc, None]
            prev_x2 = None
            prev_ad = None
            x2w_free = [None, None]
            ap_hist = [bpc, None]
            ad_hist = [None, None]
            o_writes_done = ("dout", p.cnt["dout"])
            for tc in range(NTC):
                sl = tc % 2
                ok = p.dma("sync", osb[sl][:], o_h[tc],
                           waits=[osb_free[sl], o_writes_done if tc < 2 else None])
                for c4 in range(4):
                    for dc in range(16):
                        p.emit("tensor", lambda e, sl=sl, c4=c4, dc=dc: e.matmul(
                            psD[sl][c4][:],
                            osb[sl][:, dc * 128:(dc + 1) * 128],
                            wp_sb[:, dc * E + c4 * 512: dc * E + (c4 + 1) * 512],
                            start=(dc == 0), stop=(dc == 15)),
                            waits=([ok, wpk, ps_free[sl]] if (c4 == 0 and dc == 0) else []),
                            inc=("pe" if (c4 == 3 and dc == 15) else None))
                mm = ("pe", p.cnt["pe"])
                osb_free[sl] = mm
                ev = None
                for c4 in range(4):
                    ev = p.emit("vector", lambda e, sl=sl, c4=c4: e.scalar_tensor_tensor(
                        ao_sb[sl][:, c4 * 512:(c4 + 1) * 512], psD[sl][c4][:], 1.0,
                        b_pb[:, c4 * 512:(c4 + 1) * 512], ALU.mult, ALU.add,
                        accum_out=st6[:, c4:c4 + 1]),
                        waits=[mm, ev, ap_hist[sl] if c4 == 0 else None], inc="dve")
                ps_free[sl] = ev
                sm = p.emit("vector", lambda e: e.tensor_reduce(
                    mv[:, 0:1], st6[:, 0:4], AX.X, ALU.add), waits=[ev], inc="dve")
                sq = None
                for c4 in range(4):
                    sq = p.emit("scalar", lambda e, sl=sl, c4=c4: e.activation(
                        junk[sl][:, c4 * 512:(c4 + 1) * 512], ao_sb[sl][:, c4 * 512:(c4 + 1) * 512],
                        AF.Square, bias=0.0, scale=1.0,
                        accum_out=st6[:, 4 + c4:5 + c4]),
                        waits=([ev, ad_hist[sl]] if c4 == 0 else [sq]), inc="act")
                sm2 = p.emit("vector", lambda e: e.tensor_reduce(
                    tmp2[:], st6[:, 4:8], AX.X, ALU.add), waits=[sq, sm], inc="dve")
                k1 = p.emit("vector", lambda e: e.tensor_scalar(
                    mv[:, 0:1], mv[:, 0:1], 1.0 / E, None, ALU.mult), waits=[sm2], inc="dve")
                k2 = p.emit("vector", lambda e: e.tensor_scalar(
                    tmp2[:], tmp2[:], 1.0 / E, None, ALU.mult), waits=[k1], inc="dve")
                k3 = p.emit("vector", lambda e: e.tensor_scalar(
                    mv[:, 1:2], mv[:, 0:1], mv[:, 0:1], -1.0, ALU.mult, ALU.mult),
                    waits=[k2], inc="dve")
                k4 = p.emit("vector", lambda e: e.tensor_tensor(
                    mv[:, 1:2], mv[:, 1:2], tmp2[:], ALU.add), waits=[k3], inc="dve")
                m1 = ln_tail(mv[:, 0:1], mv[:, 1:2], eps5[:], [k4])
                ap1 = p.emit("scalar", lambda e, sl=sl: e.activation(
                    junk[sl][:], ao_sb[sl][:], AF.Identity, bias=mb[:], scale=rstd[:]),
                    waits=[m1, sq], inc="act")
                ap_hist[sl] = ap1
                xk = p.dma("sync", xd_sb[:], x_h[tc * 128:(tc + 1) * 128, :],
                           waits=[prev_ad if prev_ad else B0])
                ad = p.emit("vector", lambda e, sl=sl: e.tensor_tensor(
                    x2w_sb[sl][:], junk[sl][:], xd_sb[:], ALU.add),
                    waits=[ap1, xk, x2w_free[sl]], inc="dve")
                ad_hist[sl] = ad
                prev_ad = ad
                dk = p.dma("scalar", x2_h[tc * 128:(tc + 1) * 128, :], x2w_sb[sl][:],
                           waits=[ad])
                x2w_free[sl] = dk
                prev_x2 = dk
            p.fence(("tensor", "vector", "scalar"))
        if upto == "D":
            p.wait("scalar", ("dout", p.cnt["dout"]))
            with nc.Block() as block:
                p.finalize(block)
            return nc

        with ExitStack() as stF:
            enF = stF.enter_context
            hv_sb = enF(nc.sbuf_tensor("hv_sb", [128, 4 * FF], BF16))
            xh_sb = enF(nc.sbuf_tensor("xh_sb", [128, 16 * 512], BF16))
            xct_sb = [enF(nc.sbuf_tensor(f"xct_sb{i}", [128, E], BF16)) for i in range(2)]
            wfc_sb = [enF(nc.sbuf_tensor(f"wfc{i}", [128, 16 * 512], BF16)) for i in range(2)]
            wpr_sb = [enF(nc.sbuf_tensor(f"wpr{i}", [128, FF], BF16)) for i in range(2)]
            out2_sb = enF(nc.sbuf_tensor("out2_sb", [128, 4 * E], BF16))
            u0_sb = enF(nc.sbuf_tensor("u0_sb", [128, E], F32))
            x2r_sb = enF(nc.sbuf_tensor("x2r_sb", [128, E], F32))
            yt_sb = enF(nc.sbuf_tensor("yt_sb", [128, E], F32))
            o2_sb = [enF(nc.sbuf_tensor(f"o2_{i}", [128, 512], BF16)) for i in range(2)]
            hacc = enF(nc.sbuf_tensor("hacc", [128, 64], F32))
            hsq2 = enF(nc.sbuf_tensor("hsq2", [128, 64], F32))
            sgx = enF(nc.sbuf_tensor("sgx", [128, 4], F32))
            sgv = enF(nc.sbuf_tensor("sgv", [128, 4], F32))
            stat4 = enF(nc.sbuf_tensor("stat4", [128, 8], F32))
            wst = enF(nc.sbuf_tensor("wst", [1, 256], F32))
            mnegb = enF(nc.sbuf_tensor("mnegb", [128, 4], BF16))
            junkF = u0_sb[:, 0:512]
            browF = yt_sb[0:1, 0:512]

            NT = T // 512
            B0 = p.boundary()
            x2r_free = B0
            u0_free = None
            xct_free = [None, None]
            wfc_free = [B0, B0]
            wpr_free = [B0, B0]
            psH_free = [None, None]
            psC_free = [None, None]
            out2_free = []
            o2_free = [None, None]
            hacc_free = None
            xh_free = None
            hv_free = None
            mm2_pe_done = None
            prevF_m1 = None
            prev_vchain = None
            y_toks = []
            xh_ready = None
            ring4 = []
            wst_free = [B0, B0]

            lnF_p1 = {}

            def lnF_chain(f, tau):
                nonlocal x2r_free, u0_free, prevF_m1, prev_vchain
                if f >= NT:
                    return
                tc = f * 4 + tau
                xk = p.dma("sync", x2r_sb[:], x2_h[tc * 128:(tc + 1) * 128, :],
                           waits=[x2r_free])
                s2 = ln_stats(x2r_sb, E, [xk, prevF_m1])
                m1 = ln_tail(mv[:, 0:1], mv[:, 1:2], eps5[:], [s2])
                u0k = p.emit("scalar", lambda e: e.activation(
                    u0_sb[:], x2r_sb[:], AF.Identity, bias=mb[:], scale=rstd[:]),
                    waits=[m1, u0_free], inc="act")
                x2r_free = u0k
                s2b = ln_stats(u0_sb, E, [u0k])
                a1 = p.emit("scalar", lambda e: e.activation(
                    tmp1[:], mv[:, 1:2], AF.Ln, bias=eps6[:], scale=1.0),
                    waits=[s2b], inc="act")
                a2 = p.emit("scalar", lambda e: e.activation(
                    stat4[:, 0:1], tmp1[:], AF.Exp, bias=0.0, scale=-0.5),
                    waits=[a1, prev_vchain], inc="act")
                sg = p.emit("vector", lambda e, tau=tau: e.tensor_scalar(
                    sgx[:, tau:tau + 1], stat4[:, 0:1], gam[:, 2:3], 128.0,
                    ALU.mult, ALU.mult), waits=[a2], inc="dve")
                prev_vchain = sg
                p1 = p.emit("vector", lambda e, tau=tau: e.tensor_scalar(
                    xct_sb[tau % 2][:], u0_sb[:], mv[:, 0:1], None, ALU.subtract),
                    waits=[sg, xct_free[tau % 2]], inc="dve")
                u0_free = p1
                prevF_m1 = p1
                lnF_p1[tau] = p1

            def lnF_tr(f, tau):
                nonlocal xh_ready
                if f >= NT:
                    return
                p1 = lnF_p1.pop(tau)
                for ic in range(16):
                    bk_ = 4 + (len(ring4) % 4)
                    trk = p.emit("tensor", lambda e, ic=ic, bk_=bk_, tau=tau: e.transpose(
                        pbh[bk_][:, 0:128], xct_sb[tau % 2][:, ic * 128:(ic + 1) * 128],
                        idb_sb[:]),
                        waits=[p1 if ic == 0 else None,
                               xh_free if ic == 0 else None,
                               ring4[-4] if len(ring4) >= 4 else None],
                        inc="pe")
                    ring4.append(p.emit(
                        "vector", lambda e, ic=ic, bk_=bk_, tau=tau: e.tensor_copy(
                            xh_sb[:, ic * 512 + tau * 128: ic * 512 + (tau + 1) * 128],
                            pbh[bk_][:, 0:128]), waits=[trk], inc="dve"))
                xct_free[tau % 2] = ("pe", p.cnt["pe"])
                if tau == 3:
                    xh_ready = ring4[-1]

            lnF_chain(0, 0)
            lnF_chain(0, 1)
            lnF_tr(0, 0)
            lnF_chain(0, 2)
            lnF_tr(0, 1)
            lnF_chain(0, 3)
            lnF_tr(0, 2)
            lnF_tr(0, 3)
            wfc_pre = {}
            brow_cp = None
            for f in range(NT):
                hv_cp = []
                pend_tr = [None]

                def flush_tr():
                    if pend_tr[0] is None:
                        return
                    ob_, tau_, sq_ = pend_tr[0]
                    pend_tr[0] = None
                    for j in range(4):
                        bcol = tau_ * FF + (ob_ * 4 + j) * 128
                        bk_ = 4 + (len(ring4) % 4)
                        trk = p.emit("tensor", lambda e, bcol=bcol, bk_=bk_: e.transpose(
                            pbh[bk_][:, 0:128], hv_sb[:, bcol:bcol + 128], idb_sb[:]),
                            waits=[sq_ if j == 0 else None,
                                   ring4[-4] if len(ring4) >= 4 else None],
                            inc="pe")
                        tok = p.emit("vector", lambda e, bcol=bcol, bk_=bk_: e.tensor_copy(
                            hv_sb[:, bcol:bcol + 128], pbh[bk_][:, 0:128]),
                            waits=[trk], inc="dve")
                        ring4.append(tok)
                        hv_cp.append(tok)

                for ob in range(16):
                    sl = ob % 2
                    if ob in wfc_pre:
                        wk = wfc_pre.pop(ob)
                    else:
                        wk = p.dma("sync", wfc_sb[sl][:].rearrange("p (c j) -> p c j", j=512),
                                   tfc_h.rearrange("(c p) o -> p c o", p=128)[:, :, ob * 512:(ob + 1) * 512],
                                   waits=[wfc_free[sl]])
                    for tau in range(4):
                        hsl = (ob * 4 + tau) % 2
                        for ic in range(16):
                            p.emit("tensor", lambda e, sl=sl, hsl=hsl, tau=tau, ic=ic: e.matmul(
                                pb[hsl][:], xh_sb[:, ic * 512 + tau * 128: ic * 512 + (tau + 1) * 128],
                                wfc_sb[sl][:, ic * 512:(ic + 1) * 512],
                                start=(ic == 0), stop=(ic == 15)),
                                waits=([wk, xh_ready, psH_free[hsl]] if ic == 0 else []),
                                inc=("pe" if ic == 15 else None))
                        mm = ("pe", p.cnt["pe"])
                        flush_tr()
                        sli = hv_sb[:, tau * FF + ob * 512: tau * FF + (ob + 1) * 512]
                        gk = p.emit("scalar", lambda e, hsl=hsl, tau=tau, ob=ob, sli=sli: e.activation(
                            sli, pb[hsl][:], AF.Gelu, bias=0.0, scale=sgx[:, tau:tau + 1],
                            accum_out=hacc[:, ob * 4 + tau: ob * 4 + tau + 1]),
                            waits=[mm,
                                   hacc_free if (ob == 0 and tau == 0) else None,
                                   hv_free if (ob == 0 and tau == 0) else None],
                            inc="act")
                        psH_free[hsl] = gk
                        sq = p.emit("scalar", lambda e, sli=sli, ob=ob, tau=tau: e.activation(
                            junkF, sli, AF.Square, bias=0.0, scale=1.0,
                            accum_out=hsq2[:, ob * 4 + tau: ob * 4 + tau + 1]),
                            waits=[], inc="act")
                        pend_tr[0] = (ob, tau, sq)
                    wfc_free[sl] = ("act", p.cnt["act"])
                flush_tr()
                xh_free = ("pe", p.cnt["pe"])
                mm1_act = ("act", p.cnt["act"])

                chain_tok = []
                for tau in range(4):
                    hm = p.emit("vector", lambda e, tau=tau: e.tensor_reduce(
                        mv[:, 0:1], hacc[:, tau:64:4], AX.X, ALU.add),
                        waits=([mm1_act] if tau == 0 else []) + [prevF_m1], inc="dve")
                    s1 = p.emit("vector", lambda e, tau=tau: e.tensor_reduce(
                        tmp2[:], hsq2[:, tau:64:4], AX.X, ALU.add), waits=[hm], inc="dve")
                    k1 = p.emit("vector", lambda e: e.tensor_scalar(
                        mv[:, 0:1], mv[:, 0:1], 1.0 / FF, None, ALU.mult), waits=[s1], inc="dve")
                    k2 = p.emit("vector", lambda e: e.tensor_scalar(
                        tmp2[:], tmp2[:], 1.0 / FF, None, ALU.mult), waits=[k1], inc="dve")
                    k3 = p.emit("vector", lambda e: e.tensor_scalar(
                        mv[:, 1:2], mv[:, 0:1], mv[:, 0:1], -1.0, ALU.mult, ALU.mult),
                        waits=[k2], inc="dve")
                    k4 = p.emit("vector", lambda e: e.tensor_tensor(
                        mv[:, 1:2], mv[:, 1:2], tmp2[:], ALU.add), waits=[k3], inc="dve")
                    a1 = p.emit("scalar", lambda e: e.activation(
                        tmp1[:], mv[:, 1:2], AF.Ln, bias=eps6[:], scale=1.0),
                        waits=[k4], inc="act")
                    a2 = p.emit("scalar", lambda e: e.activation(
                        stat4[:, 0:1], tmp1[:], AF.Exp, bias=0.0, scale=-0.5),
                        waits=[a1, prev_vchain], inc="act")
                    sgvk = p.emit("vector", lambda e, tau=tau: e.tensor_scalar(
                        sgv[:, tau:tau + 1], stat4[:, 0:1], gam[:, 3:4], 128.0,
                        ALU.mult, ALU.mult), waits=[a2], inc="dve")
                    prev_vchain = sgvk
                    mneg = p.emit("vector", lambda e, tau=tau: e.tensor_scalar(
                        mnegb[:, tau:tau + 1], mv[:, 0:1], -1.0, None, ALU.mult),
                        waits=[k4], inc="dve")
                    prevF_m1 = mneg
                    chain_tok.append(mneg)
                hacc_free = ("dve", p.cnt["dve"])

                brow3 = browF.rearrange("p (t o) -> p t o", t=4)
                for cc in range(16):
                    sl = cc % 2
                    if cc in (1, 4, 8, 12):
                        lnF_chain(f + 1, (cc - 1) // 4 if cc == 1 else cc // 4)
                    if cc in (2, 6, 10, 14):
                        lnF_tr(f + 1, (cc - 2) // 4)
                    if cc in (3, 9) and f + 1 < NT:
                        ob_p = (cc - 3) // 6
                        wfc_pre[ob_p] = p.dma(
                            "sync", wfc_sb[ob_p][:].rearrange("p (c j) -> p c j", j=512),
                            tfc_h.rearrange("(c p) o -> p c o", p=128)[:, :, ob_p * 512:(ob_p + 1) * 512],
                            waits=[wfc_free[ob_p]])
                    wk = p.dma("sync", wpr_sb[sl][:], tpr_h[cc], waits=[wpr_free[sl]])
                    wsk = p.dma("sync", wst[0:1, sl * 128:(sl + 1) * 128],
                                ws_h[0:1, cc * 128:(cc + 1) * 128],
                                waits=[wst_free[sl],
                                       ws_done if (f == 0 and cc < 2) else None])
                    hv3 = hv_sb[:].rearrange("p (t o) -> p t o", t=4)
                    for oc in range(64):
                        w_ = [wk, psC_free[sl]] if oc == 0 else []
                        if cc == 0 and oc % 4 == 0:
                            w_ = w_ + [hv_cp[(oc // 4) * 16 + 15]]
                        p.emit("tensor", lambda e, sl=sl, cc=cc, oc=oc, hv3=hv3: e.matmul(
                            pb[2 + sl][:].rearrange("p (t o) -> p t o", t=4),
                            wpr_sb[sl][:, oc * 128:(oc + 1) * 128],
                            hv3[:, :, oc * 128:(oc + 1) * 128],
                            start=(oc == 0), stop=False),
                            waits=w_, inc=None)
                    if cc == 0:
                        for tau in range(4):
                            p.emit("tensor", lambda e, tau=tau: e.matmul(
                                pb[0][0:1, tau * 128:(tau + 1) * 128],
                                mnegb[:, tau:tau + 1], idb_sb[:],
                                start=True, stop=True),
                                waits=[chain_tok[tau]], inc=("pe" if tau == 3 else None))
                        brow_cp = p.emit("vector", lambda e: e.tensor_copy(
                            browF, pb[0][0:1, 0:512]),
                            waits=[("pe", p.cnt["pe"]),
                                   y_toks[-1] if y_toks else None], inc="dve")
                        psH_free[0] = brow_cp
                    k1m = p.emit("tensor", lambda e, sl=sl, cc=cc: e.matmul(
                        pb[2 + sl][:].rearrange("p (t o) -> p t o", t=4),
                        wst[0:1, sl * 128:(sl + 1) * 128], brow3,
                        start=False, stop=True),
                        waits=[brow_cp, wsk], inc="pe")
                    mm = ("pe", p.cnt["pe"])
                    wpr_free[sl] = mm
                    wst_free[sl] = mm
                    ev = p.emit("vector", lambda e, sl=sl: e.tensor_copy(
                        o2_sb[sl][:], pb[2 + sl][:]), waits=[mm, o2_free[sl]], inc="dve")
                    psC_free[sl] = ev
                    for tau in range(4):
                        bk_ = 4 + (len(ring4) % 4)
                        trk = p.emit("tensor", lambda e, sl=sl, tau=tau, bk_=bk_: e.transpose(
                            pbh[bk_][:, 0:128], o2_sb[sl][:, tau * 128:(tau + 1) * 128],
                            idb_sb[:]),
                            waits=[ev if tau == 0 else None,
                                   ring4[-4] if len(ring4) >= 4 else None],
                            inc="pe")
                        cpk = p.emit("vector", lambda e, cc=cc, tau=tau, bk_=bk_: e.tensor_scalar(
                            out2_sb[:, tau * E + cc * 128: tau * E + (cc + 1) * 128],
                            pbh[bk_][:, 0:128], sgv[:, tau:tau + 1], None, ALU.mult),
                            waits=[trk] + (out2_free if (cc == 0 and tau == 0) else []),
                            inc="dve")
                        ring4.append(cpk)
                    o2_free[sl] = ("pe", p.cnt["pe"])
                hv_free = ("pe", p.cnt["pe"])
                mm2_pe_done = hv_free
                out2_ready = ring4[-1]

                out2_free = []
                fin = None
                for tau in range(4):
                    tc = f * 4 + tau
                    xk = p.dma("sync", x2r_sb[:], x2_h[tc * 128:(tc + 1) * 128, :],
                               waits=[x2r_free])
                    s2 = ln_stats(out2_sb[:, tau * E:(tau + 1) * E], E,
                                  [out2_ready, prevF_m1])
                    m1 = ln_tail(mv[:, 0:1], mv[:, 1:2], eps5[:], [s2])
                    prevF_m1 = m1
                    ap1 = p.emit("scalar", lambda e, tau=tau: e.activation(
                        yt_sb[:], out2_sb[:, tau * E:(tau + 1) * E], AF.Identity,
                        bias=mb[:], scale=rstd[:]),
                        waits=[m1, fin], inc="act")
                    out2_free.append(ap1)
                    ad = p.emit("vector", lambda e: e.tensor_tensor(
                        yt_sb[:], yt_sb[:], x2r_sb[:], ALU.add), waits=[ap1, xk], inc="dve")
                    x2r_free = ad
                    dk = p.dma("scalar", y_h[tc * 128:(tc + 1) * 128, :], yt_sb[:], waits=[ad])
                    fin = dk
                    y_toks.append(dk)

            p.wait("scalar", y_toks[-1])

        with nc.Block() as block:
            p.finalize(block)
    return nc


def prep_inputs(L0_attn_x, L0_attn_w_attn, L0_attn_b_attn, L0_attn_w_proj,
                L0_attn_b_proj, L0_ffn_w_fc, L0_ffn_w_proj, T=2048):
    f32 = np.float32
    w_attn = np.ascontiguousarray(np.asarray(L0_attn_w_attn, f32))
    b_attn = np.ascontiguousarray(np.asarray(L0_attn_b_attn, f32)).reshape(1, 3 * E)
    w_proj = np.ascontiguousarray(np.asarray(L0_attn_w_proj, f32))
    b_proj = np.ascontiguousarray(np.asarray(L0_attn_b_proj, f32)).reshape(1, E)
    wfcT = np.ascontiguousarray(np.asarray(L0_ffn_w_fc, f32).T)
    wprT = np.ascontiguousarray(np.asarray(L0_ffn_w_proj, f32).T)
    x = np.asarray(L0_attn_x, f32)
    in_maps = []
    for c in range(x.shape[0]):
        in_maps.append({
            "x": np.ascontiguousarray(x[c, :T, :]),
            "w_attn": w_attn, "b_attn": b_attn,
            "w_proj": w_proj, "b_proj": b_proj,
            "wfcT": wfcT, "wprT": wprT,
        })
    return in_maps



def kernel(L0_attn_x, L0_attn_w_attn, L0_attn_b_attn, L0_attn_w_proj,
           L0_attn_b_proj, L0_ffn_w_fc, L0_ffn_w_proj):
    from concourse.bass_utils import run_bass_kernel_spmd
    nc = build(T=2048)
    in_maps = prep_inputs(L0_attn_x, L0_attn_w_attn, L0_attn_b_attn,
                          L0_attn_w_proj, L0_attn_b_proj,
                          L0_ffn_w_fc, L0_ffn_w_proj, T=2048)
    res = run_bass_kernel_spmd(nc, in_maps, core_ids=list(range(8)))
    return np.stack([res.results[c]["y"] for c in range(8)], axis=0)


# version 39
# speedup vs baseline: 1.0431x; 1.0226x over previous
import numpy as np
import ml_dtypes
import concourse.bass as bass
import concourse.mybir as mybir
from contextlib import ExitStack

F32 = mybir.dt.float32
BF16 = mybir.dt.bfloat16
AF = mybir.ActivationFunctionType
ALU = mybir.AluOpType
AX = mybir.AxisListType

E = 2048
H = 16
FF = 8192
NEG = -1e30
ESCALE = float(1.0 / np.sqrt(2048.0))
MARGIN = -20.0
CLIP_HI = float(128.0 - 1e-6)
CLIP_LO = float(-128.0 + 1e-6)


class Prog:

    ENGINES = ("sync", "gpsimd", "scalar", "vector", "tensor")

    def __init__(self, nc, stack):
        self.nc = nc
        self.q = {e: [] for e in self.ENGINES}
        self.sems = {}
        self.cnt = {}
        self.dirty = {e: False for e in self.ENGINES}
        for name in ("din", "dgp", "dout", "pe", "dve", "act", "pool"):
            self.sems[name] = stack.enter_context(nc.semaphore(name))
            self.cnt[name] = 0

    def emit(self, eng, fn, waits=(), inc=None, k=1):
        assert fn is not None or inc is None, "nop-inc is not completion-safe"
        if fn is not None:
            self.dirty[eng] = (inc is None)
        tok = None
        if inc is not None:
            self.cnt[inc] += k
            tok = (inc, self.cnt[inc])
        best = {}
        for w in waits:
            if w is None:
                continue
            grp = w if (isinstance(w, tuple) and w and isinstance(w[0], tuple)) else (w,)
            for x in grp:
                if x is None:
                    continue
                sname, v = x
                if v > best.get(sname, -1):
                    best[sname] = v
        waits = tuple(best.items())
        sems = self.sems

        def closure(e, waits=waits, fn=fn, inc=inc, k=k):
            for s, v in waits:
                e.wait_ge(sems[s], v)
            ins = fn(e) if fn is not None else None
            if inc is not None:
                if ins is None:
                    ins = e.nop()
                ins.then_inc(sems[inc], k)
        self.q[eng].append(closure)
        return tok

    def wait(self, eng, *toks):
        toks = [t for t in toks if t is not None]
        if toks:
            self.emit(eng, None, waits=toks)

    def dma(self, eng, out, in_, waits=(), **kw):
        sem = {"sync": "din", "gpsimd": "dgp", "scalar": "dout"}[eng]
        return self.emit(eng, lambda e: e.dma_start(out=out, in_=in_, **kw),
                         waits=waits, inc=sem, k=16)

    def boundary(self):
        return tuple((s, self.cnt[s]) for s in ("pe", "dve", "act", "din", "dgp", "dout"))

    def fence(self, engines=("tensor", "vector", "scalar")):
        m = {"tensor": "pe", "vector": "dve", "scalar": "act"}
        for e in engines:
            assert not self.dirty[e], f"fence with un-inc'd tail on {e}"
        toks = [(m[e], self.cnt[m[e]]) for e in engines]
        for e in engines:
            self.wait(e, *[t for t in toks if t[0] != m[e]])

    def finalize(self, block):
        for eng in self.ENGINES:
            closures = self.q[eng]
            fn = getattr(block, eng)

            def body(engine, closures=closures):
                for c in closures:
                    c(engine)
            fn(body)


def make_consts():
    a = np.arange(64)
    freqs = 10000.0 ** (-(2 * a) / 128.0)
    RT = np.zeros((128, H * 128), np.float32)
    for h in range(H):
        ang = h * freqs
        c, s = np.cos(ang), np.sin(ang)
        R = np.zeros((128, 128), np.float32)
        R[2 * a, 2 * a] = c
        R[2 * a, 2 * a + 1] = -s
        R[2 * a + 1, 2 * a] = s
        R[2 * a + 1, 2 * a + 1] = c
        RT[:, h * 128:(h + 1) * 128] = R.T
    stair = np.where(np.tril(np.ones((128, 128), bool)), 0.0, NEG).astype(np.float32)
    return {
        "c_rt": RT.astype(ml_dtypes.bfloat16),
        "c_idb": np.eye(128).astype(ml_dtypes.bfloat16),
        "c_idf": np.ones((1, 1), dtype=np.float32),
        "c_stair": stair.astype(ml_dtypes.bfloat16),
        "c_stairT": np.ascontiguousarray(stair.T).astype(ml_dtypes.bfloat16),
        "c_onec_bf": np.ones((128, 1), ml_dtypes.bfloat16),
        "c_oner_bf": np.ones((1, 128), ml_dtypes.bfloat16),
        "c_onec_f": np.ones((128, 1), np.float32),
        "c_oner_f": np.ones((1, 128), np.float32),
    }


def build(T=2048, upto="F", debug=False):
    NS = T // 512
    NTC = T // 128
    nc = bass.Bass("TRN2", target_bir_lowering=False)

    x_h = nc.dram_tensor("x", [T, E], F32, kind="ExternalInput").ap()
    wattn_h = nc.dram_tensor("w_attn", [E, 3 * E], F32, kind="ExternalInput").ap()
    battn_h = nc.dram_tensor("b_attn", [1, 3 * E], F32, kind="ExternalInput").ap()
    wproj_h = nc.dram_tensor("w_proj", [E, E], F32, kind="ExternalInput").ap()
    bproj_h = nc.dram_tensor("b_proj", [1, E], F32, kind="ExternalInput").ap()
    wfcT_h = nc.dram_tensor("wfcT", [E, FF], F32, kind="ExternalInput").ap()
    wprT_h = nc.dram_tensor("wprT", [FF, E], F32, kind="ExternalInput").ap()
    y_h = nc.dram_tensor("y", [T, E], F32, kind="ExternalOutput").ap()

    sk = "ExternalOutput" if debug else "Internal"
    tfc_h = nc.dram_tensor("s_tfc", [E, FF], BF16, kind=sk).ap()
    tpr_h = nc.dram_tensor("s_tpr", [16, 128, FF], BF16, kind=sk).ap()
    o_h = nc.dram_tensor("s_o", [T // 128, 128, E], BF16, kind=sk).ap()
    x2_h = nc.dram_tensor("s_x2", [T, E], F32, kind=sk).ap()
    ws_h = nc.dram_tensor("s_ws", [1, E], F32, kind=sk).ap()

    cap = {k: nc.inline_tensor(v, k).ap() for k, v in make_consts().items()}

    st = ExitStack()
    with st:
        en = st.enter_context
        p = Prog(nc, st)

        pb = [en(nc.psum_tensor(f"pb{i}", [128, 512], F32)) for i in range(8)]
        pbh = [b[:].bitcast(BF16) for b in pb]

        rt_sb = en(nc.sbuf_tensor("rt_sb", [128, H * 128], BF16))
        idb_sb = en(nc.sbuf_tensor("idb_sb", [128, 128], BF16))
        idf_sb = en(nc.sbuf_tensor("idf_sb", [1, 1], F32))
        stair_sb = en(nc.sbuf_tensor("stair_sb", [128, 128], BF16))
        stairT_sb = en(nc.sbuf_tensor("stairT_sb", [128, 128], BF16))
        onec_bf = en(nc.sbuf_tensor("onec_bf", [128, 1], BF16))
        oner_bf = en(nc.sbuf_tensor("oner_bf", [1, 128], BF16))
        onec_f = en(nc.sbuf_tensor("onec_f", [128, 1], F32))
        oner_f = en(nc.sbuf_tensor("oner_f", [1, 128], F32))
        bcols = en(nc.sbuf_tensor("bcols", [128, 32], F32))
        bvrow = en(nc.sbuf_tensor("bvrow", [1, E], BF16))
        eps5 = en(nc.sbuf_tensor("eps5", [128, 1], F32))
        eps6 = en(nc.sbuf_tensor("eps6", [128, 1], F32))
        st6 = en(nc.sbuf_tensor("st6", [128, 24], F32))
        mv = en(nc.sbuf_tensor("mv", [128, 2], F32))
        rstd = en(nc.sbuf_tensor("rstd", [128, 1], F32))
        mb = en(nc.sbuf_tensor("mb", [128, 1], F32))
        tmp1 = en(nc.sbuf_tensor("tmp1", [128, 1], F32))
        tmp2 = en(nc.sbuf_tensor("tmp2", [128, 1], F32))
        gam = en(nc.sbuf_tensor("gam", [128, 4], F32))
        nth = en(nc.sbuf_tensor("nth", [128, 2], F32))
        gsc = en(nc.sbuf_tensor("gsc", [1, 2], F32))

        ck = {}
        for dst, key in [(rt_sb, "c_rt"), (idb_sb, "c_idb"), (idf_sb, "c_idf"),
                         (stair_sb, "c_stair"), (stairT_sb, "c_stairT"),
                         (onec_bf, "c_onec_bf"), (oner_bf, "c_oner_bf"),
                         (onec_f, "c_onec_f"), (oner_f, "c_oner_f")]:
            ck[key] = p.dma("sync", dst[:], cap[key])
        p.emit("vector", lambda e: e.memset(eps5[:], 1e-5), inc="dve")
        p.emit("vector", lambda e: e.memset(eps6[:], 1e-6), inc="dve")

        def ln_tail(mean_ap, var_ap, eps_ap, waits):
            a1 = p.emit("scalar", lambda e: e.activation(
                tmp1[:], var_ap, AF.Ln, bias=eps_ap, scale=1.0), waits=waits, inc="act")
            a2 = p.emit("scalar", lambda e: e.activation(
                rstd[:], tmp1[:], AF.Exp, bias=0.0, scale=-0.5), waits=[a1], inc="act")
            return p.emit("vector", lambda e: e.tensor_scalar(
                mb[:], mean_ap, rstd[:], -1.0, ALU.mult, ALU.mult), waits=[a2], inc="dve")

        def ln_stats(src_tile, W, waits):
            nblk = W // 512
            s1 = None
            for i in range(nblk):
                s1 = p.emit("vector", lambda e, i=i: e.bn_stats(
                    st6[:, i * 6:(i + 1) * 6], src_tile[:, i * 512:(i + 1) * 512]),
                    waits=(list(waits) if i == 0 else [s1]), inc="dve")
            return p.emit("vector", lambda e, nblk=nblk: e.bn_aggr(
                mv[:], st6[:, 0:6 * nblk]), waits=[s1], inc="dve")

        with ExitStack() as stT:
            enT = stT.enter_context
            brow = enT(nc.sbuf_tensor("brow", [1, 3 * E], F32))
            tw = [enT(nc.sbuf_tensor(f"tw{i}", [128, 4096], F32)) for i in range(4)]
            ta = [enT(nc.sbuf_tensor(f"ta{i}", [128, 4096], BF16)) for i in range(3)]
            tb = [enT(nc.sbuf_tensor(f"tb{i}", [128, 4096], BF16)) for i in range(3)]
            tob = [enT(nc.sbuf_tensor(f"tob{i}", [128, 4096], BF16)) for i in range(3)]
            acc = enT(nc.sbuf_tensor("acc", [128, 2], F32))

            t_b = p.dma("sync", brow[:], battn_h)
            prev_copy = None
            for j in range(32):
                tk = p.emit("tensor", lambda e, j=j: e.transpose(
                    pb[0][:, 0:1], brow[0:1, j * 128:(j + 1) * 128], idf_sb[:]),
                    waits=([t_b, ck["c_idf"]] if j == 0 else []) + [prev_copy],
                    inc="pe")
                prev_copy = p.emit("vector", lambda e, j=j: e.tensor_copy(
                    bcols[:, j:j + 1], pb[0][:, 0:1]), waits=[tk], inc="dve")
            bv = p.emit("vector", lambda e: e.tensor_copy(bvrow[:], brow[0:1, 2 * E:3 * E]),
                        waits=[t_b, prev_copy], inc="dve")

            z0 = p.emit("vector", lambda e: e.memset(acc[:], 0.0), waits=[bv], inc="dve")
            dve_last = z0
            ti = 0
            tw_free = [[None]] * 4
            red2 = enT(nc.sbuf_tensor("red2", [128, 2], F32))

            def absum(src_ap, width, col):
                nonlocal dve_last, ti
                buf = tw[ti % 4]
                tk = p.dma("sync", buf[:, 0:width], src_ap, waits=tw_free[ti % 4])
                if ti % 2 == 0:
                    rk = p.emit("scalar", lambda e, buf=buf, width=width, ti=ti: e.activation(
                        ta[ti % 2][:, 0:width], buf[:, 0:width], AF.Abs, bias=0.0, scale=1.0,
                        accum_out=red2[:, ti % 2:ti % 2 + 1]),
                        waits=[tk, dve_last if ti >= 2 else None], inc="act")
                else:
                    rk = p.emit("vector", lambda e, buf=buf, width=width, ti=ti: e.tensor_reduce(
                        red2[:, ti % 2:ti % 2 + 1], buf[:, 0:width], AX.X, ALU.add,
                        apply_absolute_value=True),
                        waits=[tk, dve_last if ti >= 2 else None], inc="dve")
                tw_free[ti % 4] = [rk]
                dve_last = p.emit("vector", lambda e, col=col, ti=ti: e.tensor_tensor(
                    acc[:, col:col + 1], acc[:, col:col + 1], red2[:, ti % 2:ti % 2 + 1],
                    ALU.add), waits=[rk], inc="dve")
                ti += 1

            for r in range(E // 128):
                for cpart in range(2):
                    absum(wfcT_h[r * 128:(r + 1) * 128, cpart * 4096:(cpart + 1) * 4096], 4096, 0)
            for r in range(FF // 128):
                absum(wprT_h[r * 128:(r + 1) * 128, :], E, 1)

            p.emit("tensor", lambda e: e.matmul(pb[0][0:1, 0:1], acc[:, 0:1],
                                                onec_f[:], start=True, stop=True),
                   waits=[dve_last, ck["c_onec_f"], prev_copy], inc=None)
            tk = p.emit("tensor", lambda e: e.matmul(pb[0][0:1, 128:129], acc[:, 1:2],
                                                     onec_f[:], start=True, stop=True),
                        inc="pe")
            g1a = p.emit("vector", lambda e: e.tensor_copy(
                gsc[0:1, 0:1], pb[0][0:1, 0:1]), waits=[tk], inc="dve")
            g1 = p.emit("vector", lambda e: e.tensor_copy(
                gsc[0:1, 1:2], pb[0][0:1, 128:129]), waits=[g1a], inc="dve")
            g2 = p.emit("vector", lambda e: e.tensor_scalar(
                gsc[0:1, 0:2], gsc[0:1, 0:2], 1.0 / (E * FF), None, ALU.mult),
                waits=[g1], inc="dve")
            bk = p.emit("tensor", lambda e: e.matmul(pb[1][:, 0:2], oner_f[0:1, :],
                                                     gsc[0:1, 0:2], start=True, stop=True),
                        waits=[g2, ck["c_oner_f"]], inc="pe")
            g3 = p.emit("vector", lambda e: e.tensor_copy(gam[:, 0:2], pb[1][:, 0:2]),
                        waits=[bk], inc="dve")
            g4 = p.emit("vector", lambda e: e.tensor_scalar(
                gam[:, 2:4], gam[:, 0:2], 1.0 / 256.0, None, ALU.mult), waits=[g3], inc="dve")
            g5 = p.emit("vector", lambda e: e.tensor_scalar(
                gam[:, 0:2], gam[:, 0:2], 1e-6, 0.5, ALU.add, ALU.mult), waits=[g4], inc="dve")
            g6 = p.emit("vector", lambda e: e.tensor_scalar(
                nth[:, 0:2], gam[:, 0:2], -1.0, None, ALU.mult), waits=[g5], inc="dve")

            BIG = 1e30
            ta_free = [None, None, None]
            tb_free = [None, None, None]
            tob_free = [[None], [None], [None]]

            def tern(src_ap, dst_fn, width, col):
                nonlocal ti
                buf = tw[ti % 4]
                tk = p.dma("sync", buf[:, 0:width], src_ap, waits=tw_free[ti % 4])
                a1 = p.emit("scalar", lambda e, buf=buf, width=width, col=col, ti=ti: e.activation(
                    ta[ti % 3][:, 0:width], buf[:, 0:width], AF.Sign,
                    bias=nth[:, col:col + 1], scale=1.0),
                    waits=[tk, g6, ta_free[ti % 3]], inc="act")
                if ti % 4 == 0:
                    b2 = p.emit("scalar", lambda e, buf=buf, width=width, col=col, ti=ti: e.activation(
                        tb[ti % 3][:, 0:width], buf[:, 0:width], AF.Sign,
                        bias=gam[:, col:col + 1], scale=1.0),
                        waits=[tb_free[ti % 3]], inc="act")
                    tw_free[ti % 4] = [b2]
                else:
                    b1 = p.emit("vector", lambda e, buf=buf, width=width, col=col, ti=ti: e.tensor_scalar(
                        tb[ti % 3][:, 0:width], buf[:, 0:width], gam[:, col:col + 1], BIG,
                        ALU.add, ALU.mult),
                        waits=[tk, g6, tb_free[ti % 3]], inc="dve")
                    b2 = p.emit("vector", lambda e, width=width, ti=ti: e.tensor_scalar(
                        tb[ti % 3][:, 0:width], tb[ti % 3][:, 0:width], 1.0, -1.0,
                        ALU.min, ALU.max), waits=[b1], inc="dve")
                    tw_free[ti % 4] = [a1, b1]
                a3 = p.emit("vector", lambda e, width=width, ti=ti: e.tensor_tensor(
                    tob[ti % 3][:, 0:width], ta[ti % 3][:, 0:width],
                    tb[ti % 3][:, 0:width], ALU.add),
                    waits=[a1, b2] + tob_free[ti % 3], inc="dve")
                ta_free[ti % 3] = a3
                tb_free[ti % 3] = a3
                tob_free[ti % 3] = dst_fn(a3, tob[ti % 3])
                ti += 1

            for r in range(E // 128):
                for cpart in range(2):
                    tern(wfcT_h[r * 128:(r + 1) * 128, cpart * 4096:(cpart + 1) * 4096],
                         lambda a2, tb, r=r, cpart=cpart: [p.dma(
                             "scalar",
                             tfc_h[r * 128:(r + 1) * 128, cpart * 4096:(cpart + 1) * 4096],
                             tb[:, 0:4096], waits=[a2])],
                         4096, 0)

            NRP = FF // 128

            def tern_pr(a2, tb, r):
                dk = p.dma(
                    "scalar",
                    tpr_h[:, :, r * 128:(r + 1) * 128].rearrange("c p j -> p c j"),
                    tb[:, 0:E].rearrange("p (c j) -> p c j", j=128), waits=[a2])
                for s4 in range(4):
                    p.emit("tensor", lambda e, tb=tb, s4=s4, r=r: e.matmul(
                        pb[2 + s4][0:1, 0:512], onec_bf[:],
                        tb[:, s4 * 512:(s4 + 1) * 512],
                        start=(r == 0), stop=(r == NRP - 1)),
                        waits=(([a2] if s4 == 0 else []) +
                               ([ck["c_onec_bf"]] if (r == 0 and s4 == 0) else [])),
                        inc=("pe" if s4 == 3 else None))
                return [dk, ("pe", p.cnt["pe"])]

            for r in range(NRP):
                tern(wprT_h[r * 128:(r + 1) * 128, :],
                     lambda a2, tb, r=r: tern_pr(a2, tb, r), E, 1)
            wsk = None
            for s4 in range(4):
                wsk = p.emit("vector", lambda e, s4=s4: e.tensor_copy(
                    brow[0:1, s4 * 512:(s4 + 1) * 512], pb[2 + s4][0:1, 0:512]),
                    waits=([("pe", p.cnt["pe"])] if s4 == 0 else [wsk]), inc="dve")
            ws_done = p.dma("scalar", ws_h[:], brow[0:1, 0:E], waits=[wsk])
            p.fence(("tensor", "vector", "scalar"))
        if upto == "T":
            with nc.Block() as block:
                p.finalize(block)
            return nc

        with ExitStack() as stA:
            enA = stA.enter_context
            K_sb = enA(nc.sbuf_tensor("K_sb", [128, H * T], BF16))
            V_sb = enA(nc.sbuf_tensor("V_sb", [128, NTC * E], BF16))
            Q_sb = enA(nc.sbuf_tensor("Q_sb", [128, H * 512], BF16))
            xn_sb = enA(nc.sbuf_tensor("xn_sb", [128, 16 * 512], BF16))
            spareA = enA(nc.sbuf_tensor("spareA", [128, 128], BF16))
            xt_sb = enA(nc.sbuf_tensor("xt_sb", [128, E], F32))
            wblk = [enA(nc.sbuf_tensor(f"wblk{i}", [128, 16 * 256], BF16)) for i in range(2)]
            E_sb = [enA(nc.sbuf_tensor(f"E_sb{i}", [128, 512], BF16)) for i in range(4)]
            o_sb = [enA(nc.sbuf_tensor(f"o_sb{i}", [128, 512], BF16)) for i in range(3)]
            sbc_sb = [enA(nc.sbuf_tensor(f"sbc_sb{i}", [128, 512], BF16)) for i in range(2)]
            negm4 = enA(nc.sbuf_tensor("negm4", [128, 4], BF16))
            negrow = [enA(nc.sbuf_tensor(f"negrow{i}", [1, 512], BF16)) for i in range(2)]
            sinvb = [enA(nc.sbuf_tensor(f"sinvb{i}", [1, 512], BF16)) for i in range(2)]

            B0 = p.boundary()
            xt_free = B0
            prev_m1 = None
            wblk_free = [B0, B0]
            o_free = [None, None, None]
            ln_apply_tok = [None] * 4
            xn_read_done = B0

            def ln_block_tau(s, tau):
                nonlocal xt_free, prev_m1, xn_read_done
                tc = s * 4 + tau
                xk = p.dma("sync", xt_sb[:], x_h[tc * 128:(tc + 1) * 128, :],
                           waits=[xt_free])
                s2 = ln_stats(xt_sb, E, [xk, prev_m1])
                m1 = ln_tail(mv[:, 0:1], mv[:, 1:2], eps5[:], [s2])
                prev_m1 = m1
                ak = p.emit("scalar", lambda e, tau=tau: e.activation(
                    xn_sb[:, tau * 2048:(tau + 1) * 2048], xt_sb[:],
                    AF.Identity, bias=mb[:], scale=rstd[:]),
                    waits=[m1, xn_read_done if tau == 0 else None], inc="act")
                ln_apply_tok[tau] = ak
                xt_free = ak

            NB_A = 64
            dstA = [(b % 16) * 4 + b // 16 for b in range(NB_A)]
            cycA, fixA = [], []
            seenA = [False] * NB_A
            for b0 in range(NB_A):
                if seenA[b0]:
                    continue
                if dstA[b0] == b0:
                    fixA.append(b0); seenA[b0] = True; continue
                cyc = [b0]; seenA[b0] = True; nxt = dstA[b0]
                while nxt != b0:
                    cyc.append(nxt); seenA[nxt] = True; nxt = dstA[nxt]
                cycA.append(cyc)

            for tau in range(4):
                ln_block_tau(0, tau)

            for s in range(NS):
                axn_copies = []

                def amove(srcb, dstb, extra=()):
                    bk_ = 2 + (len(axn_copies) % 4)
                    trk = p.emit("tensor", lambda e, srcb=srcb, bk_=bk_: e.transpose(
                        pbh[bk_][:, 0:128], xn_sb[:, srcb * 128:(srcb + 1) * 128],
                        idb_sb[:]),
                        waits=list(extra) +
                              ([axn_copies[-4]] if len(axn_copies) >= 4 else []),
                        inc="pe")
                    axn_copies.append(p.emit(
                        "vector", lambda e, dstb=dstb, bk_=bk_: e.tensor_copy(
                            xn_sb[:, dstb * 128:(dstb + 1) * 128], pbh[bk_][:, 0:128]),
                        waits=[trk], inc="dve"))

                first_wA = list(ln_apply_tok) + [ck["c_idb"]]
                for b0 in fixA:
                    amove(b0, b0, first_wA if not axn_copies else ())
                for cyc in cycA:
                    blast = cyc[-1]
                    bk_ = 2 + (len(axn_copies) % 4)
                    trk = p.emit("tensor", lambda e, blast=blast, bk_=bk_: e.transpose(
                        pbh[bk_][:, 0:128], xn_sb[:, blast * 128:(blast + 1) * 128],
                        idb_sb[:]),
                        waits=(first_wA if not axn_copies else []) +
                              ([axn_copies[-4]] if len(axn_copies) >= 4 else []),
                        inc="pe")
                    sv = p.emit("vector", lambda e, bk_=bk_: e.tensor_copy(
                        spareA[:], pbh[bk_][:, 0:128]), waits=[trk], inc="dve")
                    axn_copies.append(sv)
                    for i in range(len(cyc) - 1, 0, -1):
                        amove(cyc[i - 1], cyc[i])
                    axn_copies.append(p.emit(
                        "vector", lambda e, b0=cyc[0]: e.tensor_copy(
                            xn_sb[:, b0 * 128:(b0 + 1) * 128], spareA[:]),
                        waits=[axn_copies[-1]], inc="dve"))
                xn_ready = axn_copies[-1]
                if upto == "Aln":
                    break

                psq_free = [axn_copies[-2], axn_copies[-1]]
                rope_free = [axn_copies[-4], axn_copies[-3]]
                pend_rope = [None, None]
                esb_free = [None, None]
                psv_free = [None, None, None, None]
                for b in range(24):
                    wk = p.dma("gpsimd", wblk[b % 2][:].rearrange("p (c j) -> p c j", j=256),
                               wattn_h[:, b * 256:(b + 1) * 256].rearrange("(c p) j -> p c j", p=128),
                               waits=[wblk_free[b % 2]])
                    if b < 16:
                        for jj in range(2):
                            j = 2 * b + jj
                            h = j % 16
                            for ic in range(16):
                                p.emit("tensor", lambda e, b=b, jj=jj, ic=ic: e.matmul(
                                    pb[2 + jj][:], wblk[b % 2][:, ic * 256 + jj * 128: ic * 256 + (jj + 1) * 128],
                                    xn_sb[:, ic * 512:(ic + 1) * 512],
                                    start=(ic == 0), stop=(ic == 15)),
                                    waits=([wk, xn_ready, psq_free[jj]] if ic == 0 else []),
                                    inc=("pe" if ic == 15 else None))
                            mm = ("pe", p.cnt["pe"])
                            ev1 = p.emit("scalar", lambda e, jj=jj, j=j: e.activation(
                                E_sb[jj][:], pb[2 + jj][:], AF.Identity,
                                bias=bcols[:, j:j + 1], scale=1.0),
                                waits=[mm, esb_free[jj]], inc="act")
                            psq_free[jj] = ev1
                            if pend_rope[jj] is not None:
                                pend_rope[jj]()
                                pend_rope[jj] = None

                            def do_rope(jj=jj, h=h, j=j, b=b, ev1=ev1):
                                rk = p.emit("tensor", lambda e, jj=jj, h=h: e.matmul(
                                    pb[4 + jj][:], rt_sb[:, h * 128:(h + 1) * 128],
                                    E_sb[jj][:], start=True, stop=True),
                                    waits=[ev1, ck["c_rt"] if b == 0 else None,
                                           rope_free[jj]], inc="pe")
                                esb_free[jj] = ("pe", p.cnt["pe"])
                                if j < 16:
                                    dst = Q_sb[:, h * 512:(h + 1) * 512]
                                else:
                                    dst = K_sb[:, h * T + s * 512: h * T + (s + 1) * 512]
                                ev2 = p.emit("vector", lambda e, jj=jj, dst=dst: e.tensor_copy(
                                    dst, pb[4 + jj][:]), waits=[rk], inc="dve")
                                rope_free[jj] = ev2
                            pend_rope[jj] = do_rope
                    else:
                        for jj in range(2):
                            if pend_rope[jj] is not None:
                                pend_rope[jj]()
                                pend_rope[jj] = None
                        vb = b - 16
                        for tau in range(4):
                            bi = 2 + tau
                            for ic in range(16):
                                p.emit("tensor", lambda e, b=b, tau=tau, ic=ic, bi=bi: e.matmul(
                                    pb[bi][:, 0:256],
                                    xn_sb[:, ic * 512 + tau * 128: ic * 512 + (tau + 1) * 128],
                                    wblk[b % 2][:, ic * 256:(ic + 1) * 256],
                                    start=(ic == 0), stop=False),
                                    waits=([wk, psv_free[tau],
                                            psq_free[0] if b == 16 and tau == 0 else None,
                                            psq_free[1] if b == 16 and tau == 1 else None,
                                            rope_free[0] if b == 16 and tau == 2 else None,
                                            rope_free[1] if b == 16 and tau == 3 else None]
                                           if ic == 0 else []),
                                    inc=None)
                            p.emit("tensor", lambda e, bi=bi, vb=vb: e.matmul(
                                pb[bi][:, 0:256], oner_bf[0:1, :],
                                bvrow[0:1, vb * 256:(vb + 1) * 256],
                                start=False, stop=True), inc="pe")
                            mm = ("pe", p.cnt["pe"])
                            ev = p.emit("vector", lambda e, bi=bi, s=s, tau=tau, vb=vb: e.tensor_copy(
                                V_sb[:, (s * 4 + tau) * E + vb * 256:(s * 4 + tau) * E + (vb + 1) * 256],
                                pb[bi][:, 0:256]), waits=[mm], inc="dve")
                            psv_free[tau] = ev
                    wblk_free[b % 2] = ("pe", p.cnt["pe"])
                xn_read_done = ("pe", p.cnt["pe"])
                p.fence(("tensor", "vector", "scalar"))
                if debug and s == 0:
                    dbg_q = nc.dram_tensor("d_q", [128, H * 512], BF16, kind="ExternalOutput").ap()
                    dbg_k = nc.dram_tensor("d_k", [128, H * T], BF16, kind="ExternalOutput").ap()
                    dbg_v = nc.dram_tensor("d_v", [128, NTC * E], BF16, kind="ExternalOutput").ap()
                    dbg_xn = nc.dram_tensor("d_xn", [128, 16 * 512], BF16, kind="ExternalOutput").ap()
                    p.dma("scalar", dbg_q, Q_sb[:])
                    p.dma("scalar", dbg_k, K_sb[:])
                    p.dma("scalar", dbg_v, V_sb[:])
                    p.dma("scalar", dbg_xn, xn_sb[:])
                    p.wait("scalar", ("dout", p.cnt["dout"]))
                if upto in ("Aq", "Aln"):
                    break

                psmax_free = [None, None]
                red_tok = [None] * 4
                negm_free = [None] * 4
                negrow_ready = [None, None]
                negrow_free = [None, None]
                esb_free = [None, None, None, None]
                exp_slot = [None, None, None, None]
                exp_tok = [None, None]
                sbc_free = [None, None]
                sum_free = [None, None]
                sinvb_free = None
                o_ps_free = [None, None]
                negtr_free = [None, None]

                def plan_a1(h):
                    for tau in range(4):
                        g = 4 * s + tau
                        Ls = 128 * min((g + 2) // 2, 4)
                        bi = tau % 2
                        p.emit("tensor", lambda e, h=h, tau=tau, Ls=Ls, bi=bi: e.matmul(
                            pb[bi][:, 0:Ls],
                            Q_sb[:, h * 512 + tau * 128: h * 512 + (tau + 1) * 128],
                            K_sb[:, h * T: h * T + Ls],
                            start=True, stop=True),
                            waits=[psmax_free[bi]], inc="pe")
                        mm = ("pe", p.cnt["pe"])
                        mk = None
                        if g == 0:
                            mk = p.emit("vector", lambda e: e.tensor_tensor(
                                pb[0][:, 0:128], pb[0][:, 0:128], stair_sb[:], ALU.add),
                                waits=[mm, ck["c_stair"]], inc="dve")
                        red_tok[tau] = p.emit("vector", lambda e, Ls=Ls, bi=bi, tau=tau: e.tensor_reduce(
                            negm4[:, tau:tau + 1], pb[bi][:, 0:Ls], AX.X, ALU.max, negate=True),
                            waits=[mm, mk, negm_free[tau]], inc="dve")
                        psmax_free[bi] = red_tok[tau]

                def plan_a2(h):
                    last = None
                    for tau in range(4):
                        bi = tau % 2
                        trk = p.emit("tensor", lambda e, tau=tau, bi=bi: e.transpose(
                            pbh[bi][0:1, 512 + tau * 128: 512 + (tau + 1) * 128],
                            negm4[:, tau:tau + 1], idb_sb[:]),
                            waits=[red_tok[tau], red_tok[bi + 2], negtr_free[bi]],
                            inc="pe")
                        negm_free[tau] = trk
                        last = p.emit("vector", lambda e, h=h, tau=tau, bi=bi: e.tensor_scalar(
                            negrow[h % 2][0:1, tau * 128:(tau + 1) * 128],
                            pbh[bi][0:1, 512 + tau * 128: 512 + (tau + 1) * 128],
                            MARGIN, None, ALU.add),
                            waits=[trk, negrow_free[h % 2] if tau == 0 else None], inc="dve")
                        negtr_free[bi] = last
                        psmax_free[bi] = last
                    negrow_ready[h % 2] = last

                pend_tail = [None]

                def plan_b(h):
                    nonlocal sinvb_free
                    nk = 4 * s + 4
                    ob_ = 4 + h % 2
                    sb_ = 6 + h % 2

                    def sum_av(kc):
                        off = max(0, (kc - 4 * s) * 128)
                        sv = kc % 4
                        p.emit("tensor", lambda e, kc=kc, off=off, sv=sv: e.matmul(
                            pb[sb_][0:1, off:512], onec_bf[:], E_sb[sv][:, off:512],
                            start=(kc == 0), stop=(kc == nk - 1)),
                            waits=[exp_slot[sv], sum_free[h % 2] if kc == 0 else None],
                            inc=None)
                        p.emit("tensor", lambda e, h=h, kc=kc, off=off, sv=sv: e.matmul(
                            pb[ob_][:, off:512], V_sb[:, kc * E + h * 128: kc * E + (h + 1) * 128],
                            E_sb[sv][:, off:512],
                            start=(kc == 0), stop=(kc == nk - 1)),
                            waits=[o_ps_free[h % 2]] if kc == 0 else [], inc="pe")
                        esb_free[sv] = ("pe", p.cnt["pe"])

                    for kc in range(nk):
                        off = max(0, (kc - 4 * s) * 128)
                        sl = kc % 2
                        p.emit("tensor", lambda e, h=h, kc=kc, off=off, sl=sl: e.matmul(
                            pb[2 + sl][:, off:512],
                            K_sb[:, h * T + kc * 128: h * T + (kc + 1) * 128],
                            Q_sb[:, h * 512 + off:(h + 1) * 512],
                            start=True, stop=False),
                            waits=[exp_tok[sl],
                                   negrow_ready[h % 2] if kc == 0 else None],
                            inc=None)
                        if kc >= 4 * s:
                            p.emit("tensor", lambda e, off=off, sl=sl: e.matmul(
                                pb[2 + sl][:, off:off + 128], stair_sb[:], idb_sb[:],
                                start=False, stop=False), inc=None)
                        nmk = p.emit("tensor", lambda e, h=h, off=off, sl=sl: e.matmul(
                            pb[2 + sl][:, off:512], oner_bf[0:1, :],
                            negrow[h % 2][0:1, off:512], start=False, stop=True),
                            inc="pe")
                        sv = kc % 4
                        ek = p.emit("scalar", lambda e, off=off, sl=sl, sv=sv: e.activation(
                            E_sb[sv][:, off:512], pb[2 + sl][:, off:512], AF.Exp,
                            bias=0.0, scale=1.0), waits=[nmk, esb_free[sv]], inc="act")
                        exp_tok[sl] = ek
                        exp_slot[sv] = ek
                        if kc == 3 and pend_tail[0] is not None:
                            pend_tail[0]()
                            pend_tail[0] = None
                        if kc >= 2:
                            sum_av(kc - 2)
                    sum_av(nk - 2)
                    sum_av(nk - 1)
                    mm_last = ("pe", p.cnt["pe"])
                    def tail(h=h, s=s, ob_=ob_, sb_=sb_, mm_last=mm_last):
                        nonlocal sinvb_free
                        r1 = p.emit("vector", lambda e: e.reciprocal(
                            pb[sb_][0:1, :], pb[sb_][0:1, :]), waits=[mm_last], inc="dve")
                        r2 = p.emit("vector", lambda e: e.tensor_scalar(
                            sinvb[h % 2][0:1, :], pb[sb_][0:1, :], ESCALE, None, ALU.mult),
                            waits=[r1, sinvb_free], inc="dve")
                        bk2 = p.emit("tensor", lambda e: e.matmul(
                            pb[sb_][:], oner_bf[0:1, :], sinvb[h % 2][0:1, :],
                            start=True, stop=True), waits=[r2], inc="pe")
                        sinvb_free = bk2
                        c1 = p.emit("vector", lambda e: e.tensor_copy(
                            sbc_sb[h % 2][:], pb[sb_][:]), waits=[bk2, sbc_free[h % 2]],
                            inc="dve")
                        sum_free[h % 2] = c1
                        om = p.emit("vector", lambda e: e.tensor_tensor(
                            o_sb[h % 3][:], pb[ob_][:], sbc_sb[h % 2][:], ALU.mult),
                            waits=[c1, o_free[h % 3]], inc="dve")
                        o_ps_free[h % 2] = om
                        sbc_free[h % 2] = om
                        dk = p.dma("scalar",
                                   o_h[s * 4:(s + 1) * 4, :, h * 128:(h + 1) * 128]
                                   .rearrange("c p j -> p c j"),
                                   o_sb[h % 3][:].rearrange("p (c j) -> p c j", j=128),
                                   waits=[om])
                        o_free[h % 3] = dk
                    pend_tail[0] = tail
                    negrow_free[h % 2] = ("pe", p.cnt["pe"])

                for h in range(H):
                    plan_a1(h)
                    if h > 0:
                        plan_b(h - 1)
                    plan_a2(h)
                    if s + 1 < NS and h in (4, 7, 10, 13):
                        ln_block_tau(s + 1, (h - 4) // 3)
                plan_b(H - 1)
                if pend_tail[0] is not None:
                    pend_tail[0]()
                    pend_tail[0] = None
                p.fence(("tensor", "vector", "scalar"))
        if upto in ("A", "Aq", "Aln"):
            p.wait("scalar", ("dout", p.cnt["dout"]))
            with nc.Block() as block:
                p.finalize(block)
            return nc

        with ExitStack() as stD:
            enD = stD.enter_context
            wp_sb = enD(nc.sbuf_tensor("wp_sb", [128, 16 * E], BF16))
            osb = [enD(nc.sbuf_tensor(f"osb{i}", [128, 16 * 128], BF16)) for i in range(2)]
            ao_sb = [enD(nc.sbuf_tensor(f"ao_sb{i}", [128, E], F32)) for i in range(4)]
            junk = [enD(nc.sbuf_tensor(f"junkD{i}", [128, E], F32)) for i in range(4)]
            b_pb = enD(nc.sbuf_tensor("b_pb", [128, E], F32))
            bprow = enD(nc.sbuf_tensor("bprow", [1, E], F32))
            xd_sb = enD(nc.sbuf_tensor("xd_sb", [128, E], F32))
            x2w_sb = [enD(nc.sbuf_tensor(f"x2w_sb{i}", [128, E], F32)) for i in range(2)]
            psD = [[pb[0], pb[1], pb[2], pb[3]], [pb[4], pb[5], pb[6], pb[7]]]

            B0 = p.boundary()
            bp = p.dma("sync", bprow[:], bproj_h, waits=[B0])
            bb = None
            for c4 in range(4):
                bb = p.emit("tensor", lambda e, c4=c4: e.matmul(
                    psD[0][c4][:], oner_f[0:1, :],
                    bprow[0:1, c4 * 512:(c4 + 1) * 512], start=True, stop=True),
                    waits=[bp] if c4 == 0 else [], inc="pe")
            bpc = None
            for c4 in range(4):
                bpc = p.emit("vector", lambda e, c4=c4: e.tensor_copy(
                    b_pb[:, c4 * 512:(c4 + 1) * 512], psD[0][c4][:]), waits=[bb], inc="dve")
            wpk_list = []
            for dc in range(16):
                wpk_list.append(p.dma("gpsimd", wp_sb[:, dc * E:(dc + 1) * E],
                                wproj_h[dc * 128:(dc + 1) * 128, :],
                                waits=[B0] if dc == 0 else []))
            osb_free = [B0, B0]
            ps_free = [bpc, None]
            prev_x2 = None
            prev_ad = None
            prevD_m1 = None
            x2w_free = [None, None]
            ap_hist = [bpc, None, None, None]
            ad_hist = [None, None, None, None]
            o_writes_done = ("dout", p.cnt["dout"])
            for tc in range(NTC):
                sl = tc % 2
                al = tc % 4
                ok = p.dma("sync", osb[sl][:], o_h[tc],
                           waits=[osb_free[sl], o_writes_done if tc < 2 else None])
                for c4 in range(4):
                    for dc in range(16):
                        p.emit("tensor", lambda e, sl=sl, c4=c4, dc=dc: e.matmul(
                            psD[sl][c4][:],
                            osb[sl][:, dc * 128:(dc + 1) * 128],
                            wp_sb[:, dc * E + c4 * 512: dc * E + (c4 + 1) * 512],
                            start=(dc == 0), stop=(dc == 15)),
                            waits=(([ok, ps_free[sl]] if (c4 == 0 and dc == 0) else []) +
                                   ([wpk_list[dc]] if tc == 0 else [])),
                            inc=("pe" if (c4 == 3 and dc == 15) else None))
                mm = ("pe", p.cnt["pe"])
                osb_free[sl] = mm
                ev = None
                for c4 in range(4):
                    ev = p.emit("vector", lambda e, sl=sl, al=al, c4=c4: e.scalar_tensor_tensor(
                        ao_sb[al][:, c4 * 512:(c4 + 1) * 512], psD[sl][c4][:], 1.0,
                        b_pb[:, c4 * 512:(c4 + 1) * 512], ALU.mult, ALU.add),
                        waits=[mm, ev, ap_hist[al] if c4 == 0 else None], inc="dve")
                ps_free[sl] = ev
                s2 = ln_stats(ao_sb[al], E, [ev, prevD_m1])
                m1 = ln_tail(mv[:, 0:1], mv[:, 1:2], eps5[:], [s2])
                prevD_m1 = m1
                ap1 = p.emit("scalar", lambda e, al=al: e.activation(
                    junk[al][:], ao_sb[al][:], AF.Identity, bias=mb[:], scale=rstd[:]),
                    waits=[m1, ad_hist[al]], inc="act")
                ap_hist[al] = ap1
                xk = p.dma("gpsimd", xd_sb[:], x_h[tc * 128:(tc + 1) * 128, :],
                           waits=[prev_ad if prev_ad else B0])
                ad = p.emit("vector", lambda e, sl=sl, al=al: e.tensor_tensor(
                    x2w_sb[sl][:], junk[al][:], xd_sb[:], ALU.add),
                    waits=[ap1, xk, x2w_free[sl]], inc="dve")
                ad_hist[al] = ad
                prev_ad = ad
                dk = p.dma("scalar", x2_h[tc * 128:(tc + 1) * 128, :], x2w_sb[sl][:],
                           waits=[ad])
                x2w_free[sl] = dk
                prev_x2 = dk
            p.fence(("tensor", "vector", "scalar"))
        if upto == "D":
            p.wait("scalar", ("dout", p.cnt["dout"]))
            with nc.Block() as block:
                p.finalize(block)
            return nc

        with ExitStack() as stF:
            enF = stF.enter_context
            hv_sb = enF(nc.sbuf_tensor("hv_sb", [128, 4 * FF], BF16))
            xh_sb = enF(nc.sbuf_tensor("xh_sb", [128, 16 * 512], BF16))
            xct_sb = [enF(nc.sbuf_tensor(f"xct_sb{i}", [128, E], BF16)) for i in range(2)]
            wfc_sb = [enF(nc.sbuf_tensor(f"wfc{i}", [128, 16 * 512], BF16)) for i in range(2)]
            wpr_sb = [enF(nc.sbuf_tensor(f"wpr{i}", [128, FF], BF16)) for i in range(2)]
            out2_sb = enF(nc.sbuf_tensor("out2_sb", [128, 4 * E], BF16))
            u0_sb = enF(nc.sbuf_tensor("u0_sb", [128, E], F32))
            x2r_sb = enF(nc.sbuf_tensor("x2r_sb", [128, E], F32))
            yt_sb = enF(nc.sbuf_tensor("yt_sb", [128, E], F32))
            o2_sb = [enF(nc.sbuf_tensor(f"o2_{i}", [128, 512], BF16)) for i in range(2)]
            hacc = enF(nc.sbuf_tensor("hacc", [128, 64], F32))
            hsq2 = enF(nc.sbuf_tensor("hsq2", [128, 64], F32))
            sgx = enF(nc.sbuf_tensor("sgx", [128, 4], F32))
            sgv = enF(nc.sbuf_tensor("sgv", [128, 4], F32))
            stat4 = enF(nc.sbuf_tensor("stat4", [128, 8], F32))
            wst = enF(nc.sbuf_tensor("wst", [1, 256], F32))
            mnegb = enF(nc.sbuf_tensor("mnegb", [128, 4], BF16))
            junkF = u0_sb[:, 0:512]
            browF = yt_sb[0:1, 0:512]

            NT = T // 512
            B0 = p.boundary()
            x2r_free = B0
            u0_free = None
            xct_free = [None, None]
            wfc_free = [B0, B0]
            wpr_free = [B0, B0]
            psH_free = [None, None]
            psC_free = [None, None]
            out2_free = []
            o2_free = [None, None]
            hacc_free = None
            xh_free = None
            hv_free = None
            mm2_pe_done = None
            prevF_m1 = None
            prev_vchain = None
            y_toks = []
            xh_ready = None
            ring4 = []
            wst_free = [B0, B0]

            lnF_p1 = {}

            def lnF_chain(f, tau):
                nonlocal x2r_free, u0_free, prevF_m1, prev_vchain
                if f >= NT:
                    return
                tc = f * 4 + tau
                xk = p.dma("sync", x2r_sb[:], x2_h[tc * 128:(tc + 1) * 128, :],
                           waits=[x2r_free])
                s2 = ln_stats(x2r_sb, E, [xk, prevF_m1])
                m1 = ln_tail(mv[:, 0:1], mv[:, 1:2], eps5[:], [s2])
                u0k = p.emit("scalar", lambda e: e.activation(
                    u0_sb[:], x2r_sb[:], AF.Identity, bias=mb[:], scale=rstd[:]),
                    waits=[m1, u0_free], inc="act")
                x2r_free = u0k
                s2b = ln_stats(u0_sb, E, [u0k])
                a1 = p.emit("scalar", lambda e: e.activation(
                    tmp1[:], mv[:, 1:2], AF.Ln, bias=eps6[:], scale=1.0),
                    waits=[s2b], inc="act")
                a2 = p.emit("scalar", lambda e: e.activation(
                    stat4[:, 0:1], tmp1[:], AF.Exp, bias=0.0, scale=-0.5),
                    waits=[a1, prev_vchain], inc="act")
                sg = p.emit("vector", lambda e, tau=tau: e.tensor_scalar(
                    sgx[:, tau:tau + 1], stat4[:, 0:1], gam[:, 2:3], 128.0,
                    ALU.mult, ALU.mult), waits=[a2], inc="dve")
                prev_vchain = sg
                p1 = p.emit("vector", lambda e, tau=tau: e.tensor_scalar(
                    xct_sb[tau % 2][:], u0_sb[:], mv[:, 0:1], None, ALU.subtract),
                    waits=[sg, xct_free[tau % 2]], inc="dve")
                u0_free = p1
                prevF_m1 = p1
                lnF_p1[tau] = p1

            def lnF_tr(f, tau):
                nonlocal xh_ready
                if f >= NT:
                    return
                p1 = lnF_p1.pop(tau)
                for ic in range(16):
                    bk_ = 4 + (len(ring4) % 4)
                    trk = p.emit("tensor", lambda e, ic=ic, bk_=bk_, tau=tau: e.transpose(
                        pbh[bk_][:, 0:128], xct_sb[tau % 2][:, ic * 128:(ic + 1) * 128],
                        idb_sb[:]),
                        waits=[p1 if ic == 0 else None,
                               xh_free if ic == 0 else None,
                               ring4[-4] if len(ring4) >= 4 else None],
                        inc="pe")
                    ring4.append(p.emit(
                        "vector", lambda e, ic=ic, bk_=bk_, tau=tau: e.tensor_copy(
                            xh_sb[:, ic * 512 + tau * 128: ic * 512 + (tau + 1) * 128],
                            pbh[bk_][:, 0:128]), waits=[trk], inc="dve"))
                xct_free[tau % 2] = ("pe", p.cnt["pe"])
                if tau == 3:
                    xh_ready = ring4[-1]

            lnF_chain(0, 0)
            lnF_chain(0, 1)
            lnF_tr(0, 0)
            lnF_chain(0, 2)
            lnF_tr(0, 1)
            lnF_chain(0, 3)
            lnF_tr(0, 2)
            lnF_tr(0, 3)
            wfc_pre = {}
            finv = [None]
            out2_ready = None
            fin_pend = {}

            def fin_stats(f_, tau):
                nonlocal prevF_m1
                tc = f_ * 4 + tau
                xk = p.dma("gpsimd", x2r_sb[:], x2_h[tc * 128:(tc + 1) * 128, :],
                           waits=[x2r_free])
                s2 = ln_stats(out2_sb[:, tau * E:(tau + 1) * E], E,
                              [out2_ready, prevF_m1])
                m1 = ln_tail(mv[:, 0:1], mv[:, 1:2], eps5[:], [s2])
                prevF_m1 = m1
                fin_pend[tau] = (xk, m1)

            def fin_apply(f_, tau):
                nonlocal x2r_free
                tc = f_ * 4 + tau
                xk, m1 = fin_pend.pop(tau)
                ap1 = p.emit("scalar", lambda e, tau=tau: e.activation(
                    yt_sb[:], out2_sb[:, tau * E:(tau + 1) * E], AF.Identity,
                    bias=mb[:], scale=rstd[:]),
                    waits=[m1, finv[0]], inc="act")
                out2_free.append(ap1)
                ad = p.emit("vector", lambda e: e.tensor_tensor(
                    yt_sb[:], yt_sb[:], x2r_sb[:], ALU.add), waits=[ap1, xk], inc="dve")
                x2r_free = ad
                dk = p.dma("scalar", y_h[tc * 128:(tc + 1) * 128, :], yt_sb[:], waits=[ad])
                finv[0] = dk
                y_toks.append(dk)
            brow_cp = None
            for f in range(NT):
                hv_cp = []
                pend_tr = [None]

                def flush_tr():
                    if pend_tr[0] is None:
                        return
                    ob_, tau_, sq_ = pend_tr[0]
                    pend_tr[0] = None
                    for j in range(4):
                        bcol = tau_ * FF + (ob_ * 4 + j) * 128
                        bk_ = 4 + (len(ring4) % 4)
                        trk = p.emit("tensor", lambda e, bcol=bcol, bk_=bk_: e.transpose(
                            pbh[bk_][:, 0:128], hv_sb[:, bcol:bcol + 128], idb_sb[:]),
                            waits=[sq_ if j == 0 else None,
                                   ring4[-4] if len(ring4) >= 4 else None],
                            inc="pe")
                        tok = p.emit("vector", lambda e, bcol=bcol, bk_=bk_: e.tensor_copy(
                            hv_sb[:, bcol:bcol + 128], pbh[bk_][:, 0:128]),
                            waits=[trk], inc="dve")
                        ring4.append(tok)
                        hv_cp.append(tok)

                for ob in range(16):
                    sl = ob % 2
                    if ob in wfc_pre:
                        wk = wfc_pre.pop(ob)
                    else:
                        wk = p.dma("sync", wfc_sb[sl][:].rearrange("p (c j) -> p c j", j=512),
                                   tfc_h.rearrange("(c p) o -> p c o", p=128)[:, :, ob * 512:(ob + 1) * 512],
                                   waits=[wfc_free[sl]])
                    for tau in range(4):
                        hsl = (ob * 4 + tau) % 2
                        for ic in range(16):
                            p.emit("tensor", lambda e, sl=sl, hsl=hsl, tau=tau, ic=ic: e.matmul(
                                pb[hsl][:], xh_sb[:, ic * 512 + tau * 128: ic * 512 + (tau + 1) * 128],
                                wfc_sb[sl][:, ic * 512:(ic + 1) * 512],
                                start=(ic == 0), stop=(ic == 15)),
                                waits=([wk, xh_ready, psH_free[hsl]] if ic == 0 else []),
                                inc=("pe" if ic == 15 else None))
                        mm = ("pe", p.cnt["pe"])
                        flush_tr()
                        sli = hv_sb[:, tau * FF + ob * 512: tau * FF + (ob + 1) * 512]
                        gk = p.emit("scalar", lambda e, hsl=hsl, tau=tau, ob=ob, sli=sli: e.activation(
                            sli, pb[hsl][:], AF.Gelu, bias=0.0, scale=sgx[:, tau:tau + 1],
                            accum_out=hacc[:, ob * 4 + tau: ob * 4 + tau + 1]),
                            waits=[mm,
                                   hacc_free if (ob == 0 and tau == 0) else None,
                                   hv_free if (ob == 0 and tau == 0) else None],
                            inc="act")
                        psH_free[hsl] = gk
                        sq = p.emit("scalar", lambda e, sli=sli, ob=ob, tau=tau: e.activation(
                            junkF, sli, AF.Square, bias=0.0, scale=1.0,
                            accum_out=hsq2[:, ob * 4 + tau: ob * 4 + tau + 1]),
                            waits=[], inc="act")
                        pend_tr[0] = (ob, tau, sq)
                    wfc_free[sl] = ("act", p.cnt["act"])
                    if f >= 1 and 1 <= ob <= 9:
                        if ob % 2 == 1 and ob >= 3:
                            fin_apply(f - 1, (ob - 3) // 2)
                        if ob % 2 == 1 and ob <= 7:
                            fin_stats(f - 1, (ob - 1) // 2)
                flush_tr()
                xh_free = ("pe", p.cnt["pe"])
                mm1_act = ("act", p.cnt["act"])

                chain_tok = []
                for tau in range(4):
                    hm = p.emit("vector", lambda e, tau=tau: e.tensor_reduce(
                        mv[:, 0:1], hacc[:, tau:64:4], AX.X, ALU.add),
                        waits=([mm1_act] if tau == 0 else []) + [prevF_m1], inc="dve")
                    s1 = p.emit("vector", lambda e, tau=tau: e.tensor_reduce(
                        tmp2[:], hsq2[:, tau:64:4], AX.X, ALU.add), waits=[hm], inc="dve")
                    k1 = p.emit("vector", lambda e: e.tensor_scalar(
                        mv[:, 0:1], mv[:, 0:1], 1.0 / FF, None, ALU.mult), waits=[s1], inc="dve")
                    k2 = p.emit("vector", lambda e: e.tensor_scalar(
                        tmp2[:], tmp2[:], 1.0 / FF, None, ALU.mult), waits=[k1], inc="dve")
                    k3 = p.emit("vector", lambda e: e.tensor_scalar(
                        mv[:, 1:2], mv[:, 0:1], mv[:, 0:1], -1.0, ALU.mult, ALU.mult),
                        waits=[k2], inc="dve")
                    k4 = p.emit("vector", lambda e: e.tensor_tensor(
                        mv[:, 1:2], mv[:, 1:2], tmp2[:], ALU.add), waits=[k3], inc="dve")
                    a1 = p.emit("scalar", lambda e: e.activation(
                        tmp1[:], mv[:, 1:2], AF.Ln, bias=eps6[:], scale=1.0),
                        waits=[k4], inc="act")
                    a2 = p.emit("scalar", lambda e: e.activation(
                        stat4[:, 0:1], tmp1[:], AF.Exp, bias=0.0, scale=-0.5),
                        waits=[a1, prev_vchain], inc="act")
                    sgvk = p.emit("vector", lambda e, tau=tau: e.tensor_scalar(
                        sgv[:, tau:tau + 1], stat4[:, 0:1], gam[:, 3:4], 128.0,
                        ALU.mult, ALU.mult), waits=[a2], inc="dve")
                    prev_vchain = sgvk
                    mneg = p.emit("vector", lambda e, tau=tau: e.tensor_scalar(
                        mnegb[:, tau:tau + 1], mv[:, 0:1], -1.0, None, ALU.mult),
                        waits=[k4], inc="dve")
                    prevF_m1 = mneg
                    chain_tok.append(mneg)
                hacc_free = ("dve", p.cnt["dve"])

                out2_consumed = list(out2_free)
                del out2_free[:]
                brow3 = browF.rearrange("p (t o) -> p t o", t=4)
                for cc in range(16):
                    sl = cc % 2
                    if cc in (1, 4, 8, 12):
                        lnF_chain(f + 1, (cc - 1) // 4 if cc == 1 else cc // 4)
                    if cc in (2, 6, 10, 14):
                        lnF_tr(f + 1, (cc - 2) // 4)
                    if cc in (3, 9) and f + 1 < NT:
                        ob_p = (cc - 3) // 6
                        wfc_pre[ob_p] = p.dma(
                            "sync", wfc_sb[ob_p][:].rearrange("p (c j) -> p c j", j=512),
                            tfc_h.rearrange("(c p) o -> p c o", p=128)[:, :, ob_p * 512:(ob_p + 1) * 512],
                            waits=[wfc_free[ob_p]])
                    wk = p.dma("sync", wpr_sb[sl][:], tpr_h[cc], waits=[wpr_free[sl]])
                    wsk = p.dma("sync", wst[0:1, sl * 128:(sl + 1) * 128],
                                ws_h[0:1, cc * 128:(cc + 1) * 128],
                                waits=[wst_free[sl],
                                       ws_done if (f == 0 and cc < 2) else None])
                    hv3 = hv_sb[:].rearrange("p (t o) -> p t o", t=4)
                    for oc in range(64):
                        w_ = [wk, psC_free[sl]] if oc == 0 else []
                        if cc == 0 and oc % 4 == 0:
                            w_ = w_ + [hv_cp[(oc // 4) * 16 + 15]]
                        p.emit("tensor", lambda e, sl=sl, cc=cc, oc=oc, hv3=hv3: e.matmul(
                            pb[2 + sl][:].rearrange("p (t o) -> p t o", t=4),
                            wpr_sb[sl][:, oc * 128:(oc + 1) * 128],
                            hv3[:, :, oc * 128:(oc + 1) * 128],
                            start=(oc == 0), stop=False),
                            waits=w_, inc=None)
                    if cc == 0:
                        for tau in range(4):
                            p.emit("tensor", lambda e, tau=tau: e.matmul(
                                pb[0][0:1, tau * 128:(tau + 1) * 128],
                                mnegb[:, tau:tau + 1], idb_sb[:],
                                start=True, stop=True),
                                waits=[chain_tok[tau]], inc=("pe" if tau == 3 else None))
                        brow_cp = p.emit("vector", lambda e: e.tensor_copy(
                            browF, pb[0][0:1, 0:512]),
                            waits=[("pe", p.cnt["pe"]),
                                   y_toks[-1] if y_toks else None], inc="dve")
                        psH_free[0] = brow_cp
                    k1m = p.emit("tensor", lambda e, sl=sl, cc=cc: e.matmul(
                        pb[2 + sl][:].rearrange("p (t o) -> p t o", t=4),
                        wst[0:1, sl * 128:(sl + 1) * 128], brow3,
                        start=False, stop=True),
                        waits=[brow_cp, wsk], inc="pe")
                    mm = ("pe", p.cnt["pe"])
                    wpr_free[sl] = mm
                    wst_free[sl] = mm
                    ev = p.emit("vector", lambda e, sl=sl: e.tensor_copy(
                        o2_sb[sl][:], pb[2 + sl][:]), waits=[mm, o2_free[sl]], inc="dve")
                    psC_free[sl] = ev
                    for tau in range(4):
                        bk_ = 4 + (len(ring4) % 4)
                        trk = p.emit("tensor", lambda e, sl=sl, tau=tau, bk_=bk_: e.transpose(
                            pbh[bk_][:, 0:128], o2_sb[sl][:, tau * 128:(tau + 1) * 128],
                            idb_sb[:]),
                            waits=[ev if tau == 0 else None,
                                   ring4[-4] if len(ring4) >= 4 else None],
                            inc="pe")
                        cpk = p.emit("vector", lambda e, cc=cc, tau=tau, bk_=bk_: e.tensor_scalar(
                            out2_sb[:, tau * E + cc * 128: tau * E + (cc + 1) * 128],
                            pbh[bk_][:, 0:128], sgv[:, tau:tau + 1], None, ALU.mult),
                            waits=[trk] + (out2_consumed if (cc == 0 and tau == 0) else []),
                            inc="dve")
                        ring4.append(cpk)
                    o2_free[sl] = ("pe", p.cnt["pe"])
                hv_free = ("pe", p.cnt["pe"])
                mm2_pe_done = hv_free
                out2_ready = ring4[-1]

            fin_stats(NT - 1, 0)
            for tau in range(4):
                fin_apply(NT - 1, tau)
                if tau < 3:
                    fin_stats(NT - 1, tau + 1)

            p.wait("scalar", y_toks[-1])

        with nc.Block() as block:
            p.finalize(block)
    return nc


def prep_inputs(L0_attn_x, L0_attn_w_attn, L0_attn_b_attn, L0_attn_w_proj,
                L0_attn_b_proj, L0_ffn_w_fc, L0_ffn_w_proj, T=2048):
    f32 = np.float32
    w_attn = np.ascontiguousarray(np.asarray(L0_attn_w_attn, f32))
    b_attn = np.ascontiguousarray(np.asarray(L0_attn_b_attn, f32)).reshape(1, 3 * E)
    w_proj = np.ascontiguousarray(np.asarray(L0_attn_w_proj, f32))
    b_proj = np.ascontiguousarray(np.asarray(L0_attn_b_proj, f32)).reshape(1, E)
    wfcT = np.ascontiguousarray(np.asarray(L0_ffn_w_fc, f32).T)
    wprT = np.ascontiguousarray(np.asarray(L0_ffn_w_proj, f32).T)
    x = np.asarray(L0_attn_x, f32)
    in_maps = []
    for c in range(x.shape[0]):
        in_maps.append({
            "x": np.ascontiguousarray(x[c, :T, :]),
            "w_attn": w_attn, "b_attn": b_attn,
            "w_proj": w_proj, "b_proj": b_proj,
            "wfcT": wfcT, "wprT": wprT,
        })
    return in_maps





def kernel(L0_attn_x, L0_attn_w_attn, L0_attn_b_attn, L0_attn_w_proj,
           L0_attn_b_proj, L0_ffn_w_fc, L0_ffn_w_proj):
    from concourse.bass_utils import run_bass_kernel_spmd
    nc = build(T=2048)
    in_maps = prep_inputs(L0_attn_x, L0_attn_w_attn, L0_attn_b_attn,
                          L0_attn_w_proj, L0_attn_b_proj,
                          L0_ffn_w_fc, L0_ffn_w_proj, T=2048)
    res = run_bass_kernel_spmd(nc, in_maps, core_ids=list(range(8)))
    return np.stack([res.results[c]["y"] for c in range(8)], axis=0)
